# version 02 (final)
# baseline (speedup 1.0000x reference)
import numpy as np
import concourse.bass as bass
import concourse.mybir as mybir
from concourse.bass_utils import run_bass_kernel_spmd

F32 = mybir.dt.float32
BF16 = mybir.dt.bfloat16
AF = mybir.ActivationFunctionType
ALU = mybir.AluOpType
AX = mybir.AxisListType

T = 2048
D = 1024
NT = T // 128
ND = D // 128
NH = 16
HS = 64
NEG = -1.0e30
EPS = 1e-5

_END_PHASE = 99
_DBG = False


def _host_constants():
    c = {}
    c['ident'] = np.eye(128, dtype=np.float32)
    c['ones_row'] = np.ones((1, 128), dtype=np.float32)
    r = np.arange(128)
    c['mask_tri'] = np.where(r[None, :] <= r[:, None], 0.0, NEG).astype(np.float32)
    mc = np.full((128, 128), NEG, np.float32)
    cc = np.arange(128)
    mc[:, :127] = np.where(cc[None, :127] <= r[:, None], 0.0, NEG)
    c['mask_comp'] = mc
    dim = 32
    inv_freq = 1.0 / (10000.0 ** (np.arange(0, dim, 2, dtype=np.float64) / dim))
    freqs = np.arange(T, dtype=np.float64)[:, None] * inv_freq[None, :]
    emb = np.concatenate([freqs, freqs], axis=-1)
    c['ropecosT'] = np.ascontiguousarray(np.cos(emb).T.astype(np.float32))
    c['ropesinT'] = np.ascontiguousarray(np.sin(emb).T.astype(np.float32))
    R = np.zeros((32, 32), dtype=np.float32)
    for i in range(16):
        R[2 * i, 2 * i + 1] = -1.0
        R[2 * i + 1, 2 * i] = 1.0
    c['RT'] = np.ascontiguousarray(R.T)
    Rbig = np.zeros((128, 128), dtype=np.float32)
    Rbig[32:64, 32:64] = R
    Rbig[96:128, 96:128] = R
    c['RbigT'] = np.ascontiguousarray(Rbig.T)
    cosF = np.ones((128, T), dtype=np.float32)
    sinF = np.zeros((128, T), dtype=np.float32)
    cosF[32:64] = c['ropecosT']
    cosF[96:128] = c['ropecosT']
    sinF[32:64] = c['ropesinT']
    sinF[96:128] = c['ropesinT']
    c['cosF'] = cosF
    c['sinF'] = sinF
    return c


def _prep_weights(inp):
    w = {}
    w['c_attn_w'] = inp['L0_attn_c_attn_w']
    w['cab'] = np.ascontiguousarray(inp['L0_attn_c_attn_b'].reshape(24, 128))
    w['c_proj_w'] = inp['L0_attn_c_proj_w']
    w['cpb'] = np.ascontiguousarray(inp['L0_attn_c_proj_b'].reshape(1, 1024))
    w['convw'] = np.ascontiguousarray(inp['L0_attn_conv_w'].transpose(2, 1, 0))
    w['convb'] = np.ascontiguousarray(np.tile(inp['L0_attn_conv_b'], 2).reshape(128, 1))
    w['compw'] = np.ascontiguousarray(inp['L0_attn_comp_ln_w'].reshape(1, 64))
    w['compb'] = np.ascontiguousarray(inp['L0_attn_comp_ln_b'].reshape(1, 64))
    w['lnlw'] = np.ascontiguousarray(inp['L0_attn_ln_local_w'].reshape(1, 1024))
    w['lnlb'] = np.ascontiguousarray(inp['L0_attn_ln_local_b'].reshape(1, 1024))
    w['gw1'] = inp['L0_attn_gate_w1']
    w['gb1'] = np.ascontiguousarray(inp['L0_attn_gate_b1'].reshape(1, 1024))
    w['glnw'] = np.ascontiguousarray(inp['L0_attn_gate_ln_w'].reshape(1, 1024))
    w['glnb'] = np.ascontiguousarray(inp['L0_attn_gate_ln_b'].reshape(1, 1024))
    w['gw2'] = inp['L0_attn_gate_w2']
    w['gb2'] = np.ascontiguousarray(inp['L0_attn_gate_b2'].reshape(1, 2))
    w['w1T'] = np.ascontiguousarray(inp['L0_ffn_w1'].T)
    w['fconvw'] = np.ascontiguousarray(inp['L0_ffn_conv_w'].reshape(32, 128, 3))
    w['fconvb'] = np.ascontiguousarray(inp['L0_ffn_conv_b'].reshape(32, 128))
    w['w2T'] = np.ascontiguousarray(inp['L0_ffn_w2'].T)
    w['Wq_down'] = inp['L1_attn_Wq_down']
    w['Wk_down'] = inp['L1_attn_Wk_down']
    w['Wv_down'] = inp['L1_attn_Wv_down']
    w['Wq_up_c'] = inp['L1_attn_Wq_up_c']
    w['Wq_up_e'] = inp['L1_attn_Wq_up_e']
    w['Wk_up_c'] = inp['L1_attn_Wk_up_c']
    w['Wk_up_e'] = inp['L1_attn_Wk_up_e']
    w['Wv_up'] = inp['L1_attn_Wv_up']
    w['Wc'] = inp['L1_attn_Wc']
    def blk2(wc, we):
        b = np.zeros((128, 128), dtype=np.float32)
        b[0:64, 0:32] = wc
        b[0:64, 32:64] = we
        b[64:128, 64:96] = wc
        b[64:128, 96:128] = we
        return b
    w['Wq_blk'] = blk2(inp['L1_attn_Wq_up_c'], inp['L1_attn_Wq_up_e'])
    w['Wk_blk'] = blk2(inp['L1_attn_Wk_up_c'], inp['L1_attn_Wk_up_e'])
    wv = np.zeros((128, 128), dtype=np.float32)
    wv[0:64, 0:64] = inp['L1_attn_Wv_up']
    wv[64:128, 64:128] = inp['L1_attn_Wv_up']
    w['Wv_blk'] = wv
    w['rwT'] = np.ascontiguousarray(inp['L1_ffn_router_weight'].T)
    w['rbias'] = np.ascontiguousarray(inp['L1_ffn_router_bias'].reshape(1, 32))
    w['gateT'] = np.ascontiguousarray(inp['L1_ffn_gate_w'].transpose(0, 2, 1))
    w['upT'] = np.ascontiguousarray(inp['L1_ffn_up_w'].transpose(0, 2, 1))
    w['downT'] = np.ascontiguousarray(inp['L1_ffn_down_w'].transpose(0, 2, 1))
    w['sgT'] = np.ascontiguousarray(inp['L1_ffn_shared_gate_w'].T)
    w['suT'] = np.ascontiguousarray(inp['L1_ffn_shared_up_w'].T)
    w['sdT'] = np.ascontiguousarray(inp['L1_ffn_shared_down_w'].T)
    return w



QSEMS = 4


class Emit:

    def __init__(self, nc):
        self.nc = nc
        self._sem_cms = []
        self.sems = {}
        self.count = {}
        self.eng = {'pe': nc.tensor, 've': nc.vector, 'ac': nc.scalar,
                    'gp': nc.gpsimd, 'sy': nc.sync}
        self.waited = {e: {} for e in self.eng}
        names = ['pe', 've', 'ac', 'gp']
        self.qsems = {}
        for q in ('dsy', 'dgp', 'dsc'):
            self.qsems[q] = ['%s%d' % (q, i) for i in range(QSEMS)]
            names += self.qsems[q]
        for name in names:
            cm = nc.semaphore(name + '_sem')
            self.sems[name] = cm.__enter__()
            self._sem_cms.append(cm)
            self.count[name] = 0
        self.qidx = {q: 0 for q in self.qsems}

    def qtail(self, q):
        return [(s, self.count[s]) for s in self.qsems[q]]

    def after(self, ename, deps):
        e = self.eng[ename]
        w = self.waited[ename]
        for s, v in deps:
            if v is None or v <= 0:
                continue
            if w.get(s, 0) < v:
                e.wait_ge(self.sems[s], v)
                w[s] = v

    def prod(self, inst, sname, k=1):
        if sname == 'pe' and getattr(self, 'pe_mode', ('',))[0] == 'tr':
            self.eng['pe'].drain()
            inst = self.eng['pe'].nop()
            self.pe_mode = ('drained',)
        self.count[sname] += k
        inst.then_inc(self.sems[sname], k)
        if sname in ('ve', 'ac', 'gp'):
            self.eng[sname].wait_ge(self.sems[sname], self.count[sname])
            self.waited[sname][sname] = self.count[sname]
        return (sname, self.count[sname])

    def vesync(self):
        self.after('ve', [self.tail('ve')])

    def tail(self, sname):
        return (sname, self.count[sname])


def _aslist(dep):
    if dep is None:
        return []
    if isinstance(dep, list):
        return dep
    return [dep]


class Slot:
    __slots__ = ('ready', 'users')

    def __init__(self):
        self.ready = []
        self.users = []

    def rd(self, em, ename):
        em.after(ename, self.ready)

    def used(self, dep):
        self.users += _aslist(dep)

    def wr(self, em, ename):
        em.after(ename, self.users + self.ready)

    def set(self, dep):
        self.ready = _aslist(dep)
        self.users = []



def _pe_mode(em, kind, K, M):
    r = (lambda v: 32 if v <= 32 else (64 if v <= 64 else 128))
    mode = (kind, r(K), r(M))
    prev = getattr(em, 'pe_mode', None)
    if prev is not None and prev != mode and prev != ('drained',):
        em.eng['pe'].drain()
    em.pe_mode = mode


def MM(em, out, lhsT, rhs, **kw):
    _pe_mode(em, 'mm', lhsT.shape[0], lhsT.shape[1])
    return em.eng['pe'].matmul(out, lhsT, rhs, **kw)


def TR(em, out, in_, ident):
    _pe_mode(em, 'tr', in_.shape[0], in_.shape[1])
    return em.eng['pe'].transpose(out, in_, ident)


DMA_ENG = {'dsy': 'sy', 'dgp': 'gp', 'dsc': 'ac'}


def dma(em, q, out_ap, in_ap, dst_slots=(), src_slots=(), slow=False):
    ename = DMA_ENG[q]
    sname = em.qsems[q][em.qidx[q] % QSEMS]
    em.qidx[q] += 1
    em.after(ename, [(sname, em.count[sname])])
    for s in src_slots:
        s.rd(em, ename)
    for s in dst_slots:
        s.wr(em, ename)
    inst = em.eng[ename].dma_start(out_ap, in_ap, allow_slow_non_contiguous=slow)
    dep = em.prod(inst, sname, 16)
    for s in src_slots:
        s.used(dep)
    for s in dst_slots:
        s.set(dep)
    return dep



def build_nc():
    nc = bass.Bass('TRN2', target_bir_lowering=False, detect_race_conditions=False)
    hc = _host_constants()

    IN = {}

    def din(name, shape):
        IN[name] = nc.dram_tensor(name, list(shape), F32, kind='ExternalInput')
        return IN[name]

    din('x_tm', (T, D))
    din('x_dm', (D, T))
    for n, a in [('c_attn_w', (1024, 3072)), ('cab', (24, 128)), ('c_proj_w', (1024, 1024)),
                 ('cpb', (1, 1024)), ('convw', (32, 64, 64)), ('convb', (128, 1)),
                 ('compw', (1, 64)), ('compb', (1, 64)), ('lnlw', (1, 1024)), ('lnlb', (1, 1024)),
                 ('gw1', (2048, 1024)), ('gb1', (1, 1024)), ('glnw', (1, 1024)), ('glnb', (1, 1024)),
                 ('gw2', (1024, 2)), ('gb2', (1, 2)),
                 ('w1T', (1024, 4096)), ('fconvw', (32, 128, 3)), ('fconvb', (32, 128)),
                 ('w2T', (4096, 1024)),
                 ('Wq_down', (1024, 1024)), ('Wk_down', (1024, 1024)), ('Wv_down', (1024, 1024)),
                 ('Wq_up_c', (64, 32)), ('Wq_up_e', (64, 32)), ('Wk_up_c', (64, 32)),
                 ('Wk_up_e', (64, 32)), ('Wv_up', (64, 64)), ('Wc', (1024, 1024)),
                 ('Wq_blk', (128, 128)), ('Wk_blk', (128, 128)), ('Wv_blk', (128, 128)),
                 ('RbigT', (128, 128)), ('cosF', (128, 2048)), ('sinF', (128, 2048)),
                 ('rwT', (1024, 32)), ('rbias', (1, 32)),
                 ('gateT', (32, 1024, 512)), ('upT', (32, 1024, 512)), ('downT', (32, 512, 1024)),
                 ('sgT', (1024, 1024)), ('suT', (1024, 1024)), ('sdT', (1024, 1024)),
                 ('ident', (128, 128)), ('ones_row', (1, 128)), ('mask_tri', (128, 128)),
                 ('mask_comp', (128, 128)), ('ropecosT', (32, 2048)), ('ropesinT', (32, 2048)),
                 ('RT', (32, 32))]:
        din(n, a)

    OUT = nc.dram_tensor('out', [T, D], F32, kind='ExternalOutput')
    if _DBG:
        for nm, shp in [('dbg_loc', [128, NT * 1024]), ('dbg_cmp', [128, NT * 1024]),
                        ('dbg_q', [128, 2048]), ('dbg_k', [128, 2048]),
                        ('dbg_vp', [128, 2048]), ('dbg_kc', [128, 128]),
                        ('dbg_vc', [128, 128]), ('dbg_g', [1, 8]),
                        ('dbg_co', [128, 128]), ('dbg_cot', [128, 64]),
                        ('dbg_cot2', [128, 64]), ('dbg_cst', [128, 16]),
                        ('dbg_sc', [128, 512]), ('dbg_wd', [128, 512]),
                        ('dbg_logi', [128, 512])]:
            IN[nm] = nc.dram_tensor(nm, shp, F32, kind='ExternalOutput')
    x1_hbm = nc.dram_tensor('x1_scr', [T, D], F32, kind='Internal')
    x2_hbm = nc.dram_tensor('x2_scr', [T, D], F32, kind='Internal')
    w1q_hbm = nc.dram_tensor('w1q_scr', [32, 8, 128, 128], BF16, kind='Internal')
    w2q_hbm = nc.dram_tensor('w2q_scr', [32, 128, 1024], BF16, kind='Internal')
    nc._row_scr = nc.dram_tensor('row_scr', [2048], F32, kind='Internal')

    em = Emit(nc)
    PE, VE, AC, GP, SY = nc.tensor, nc.vector, nc.scalar, nc.gpsimd, nc.sync

    persist = [
        nc.sbuf_tensor('XT', [128, ND * T], BF16),
        nc.sbuf_tensor('idb', [128, 128], BF16),
        nc.sbuf_tensor('idf', [128, 128], F32),
        nc.sbuf_tensor('onesf', [1, 128], F32),
        nc.sbuf_tensor('onesb', [1, 128], BF16),
        nc.sbuf_tensor('mtri', [128, 128], BF16),
        nc.sbuf_tensor('STAT', [128, 512], F32),
        nc.sbuf_tensor('ROWS', [1, 1088], F32),
        nc.sbuf_tensor('invsxrow', [1, 2048], F32),
        nc.sbuf_tensor('isx', [128, 16], F32),
        nc.sbuf_tensor('sx', [128, 16], F32),
    ]
    persist_l0 = [
        nc.sbuf_tensor('mcomp', [128, 128], BF16),
        nc.sbuf_tensor('cab_s', [128, 24], F32),
        nc.sbuf_tensor('convw_s', [128, 32 * 64], BF16),
        nc.sbuf_tensor('convb_s', [128, 1], F32),
        nc.sbuf_tensor('compwb_bc', [128, 128], F32),
        nc.sbuf_tensor('lnl_bc', [128, 2048], F32),
        nc.sbuf_tensor('cpb_bc', [128, 1024], F32),
        nc.sbuf_tensor('vb_bc', [128, 1024], F32),
    ]
    ctx_stack = []
    ctx_l0 = []
    P = {}
    for cm in persist:
        h = cm.__enter__()
        ctx_stack.append(cm)
        P[h.name] = h
    for cm in persist_l0:
        h = cm.__enter__()
        ctx_l0.append(cm)
        P[h.name] = h

    XT = P['XT']
    xt_slot = [Slot() for _ in range(NT)]

    with (
        nc.psum_tensor('pp0', [128, 512], F32) as pp0,
        nc.psum_tensor('pp1', [128, 1024], F32) as pp1,
        nc.Block() as blk,
    ):
        @blk.sync
        def _(sync):
            d = dma(em, 'dgp', XT[:],
                    bass.AP(IN['x_dm'], 0, [[T, 128], [128 * T, ND], [1, T]]))
            for s in xt_slot:
                s.set(d)
            loads = [
                (P['idf'][:], IN['ident'].ap()), (P['onesf'][:], IN['ones_row'].ap()),
                (P['convb_s'][:], IN['convb'].ap()),
            ]
            for o, i in loads:
                dma(em, 'dsy', o, i)
            dma(em, 'dsy', P['cab_s'][:], bass.AP(IN['cab'], 0, [[1, 128], [128, 24]]),
                slow=True)
            casts = [
                (P['idb'][:], IN['ident'].ap()), (P['onesb'][:], IN['ones_row'].ap()),
                (P['mtri'][:], IN['mask_tri'].ap()), (P['mcomp'][:], IN['mask_comp'].ap()),
                (P['convw_s'][0:64, :], bass.AP(IN['convw'], 0, [[64, 64], [64 * 64, 32], [1, 64]])),
                (P['convw_s'][64:128, :], bass.AP(IN['convw'], 0, [[64, 64], [64 * 64, 32], [1, 64]])),
            ]
            for o, i in casts:
                dma(em, 'dgp', o, i)
            rowsl = Slot()

            def bc(dst_ap, row_in_ap, n, ps):
                rowsl.wr(em, 'sy')
                dr = dma(em, 'dsy', P['ROWS'][0:1, 0:n], row_in_ap)
                rowsl.set(dr)
                em.after('pe', [dr])
                nchunk = (n + 511) // 512
                for i in range(nchunk):
                    w_ = min(512, n - i * 512)
                    inst = MM(em, ps[:, i * 512:i * 512 + w_],
                                     P['onesf'][:],
                                     P['ROWS'][0:1, i * 512:i * 512 + w_],
                                     start=True, stop=True)
                dep = em.prod(inst, 'pe')
                rowsl.used(dep)
                em.after('ve', [dep])
                cp = VE.tensor_copy(dst_ap, ps[:, 0:n])
                dep2 = em.prod(cp, 've')
                em.after('pe', [dep2])

            bc(P['compwb_bc'][:, 0:64], IN['compw'].ap(), 64, pp0)
            bc(P['compwb_bc'][:, 64:128], IN['compb'].ap(), 64, pp0)
            bc(P['lnl_bc'][:, 0:1024], IN['lnlw'].ap(), 1024, pp1)
            bc(P['lnl_bc'][:, 1024:2048], IN['lnlb'].ap(), 1024, pp1)
            bc(P['cpb_bc'][:], IN['cpb'].ap(), 1024, pp1)
            bc(P['vb_bc'][:], bass.AP(IN['cab'], 16 * 128, [[1024, 1], [1, 1024]]),
               1024, pp1)
            em.prod(VE.memset(P['STAT'][:, 511:512], 128.5), 've')

    _phase1(nc, em, P, IN, xt_slot, x1_hbm)

    if _END_PHASE <= 1:
        _copy_out(nc, em, P, x1_hbm, OUT)
        _final(nc, em)
        return nc

    _phase2(nc, em, P, IN, xt_slot, x1_hbm, x2_hbm, w1q_hbm, w2q_hbm)
    if _END_PHASE <= 2:
        _copy_out(nc, em, P, x2_hbm, OUT)
        _final(nc, em)
        return nc

    for cm in reversed(ctx_l0):
        cm.__exit__(None, None, None)
    x3_hbm = x1_hbm
    _phase3(nc, em, P, IN, xt_slot, x2_hbm, x3_hbm)
    if _END_PHASE <= 3:
        _copy_out(nc, em, P, x3_hbm, OUT)
        _final(nc, em)
        return nc

    _phase4(nc, em, P, IN, xt_slot, x3_hbm, OUT)
    _final(nc, em)
    return nc


def _final(nc, em):
    with nc.Block() as blk:
        @blk.sync
        def _(sync):
            em.after('sy', em.qtail('dsy') + em.qtail('dgp') + em.qtail('dsc'))


def _copy_out(nc, em, P, src_hbm, OUT):
    with nc.sbuf_tensor('cpy', [128, 2048], F32) as cp, nc.Block() as blk:
        @blk.sync
        def _(sync):
            em.after('sy', getattr(em, 'x1_deps', []) + getattr(em, 'x2_deps', [])
                + getattr(em, 'x3_deps', []))
            s = Slot()
            for t in range(NT):
                half = t % 2
                ap = cp[:, half * 1024:(half + 1) * 1024]
                d = dma(em, 'dsy', ap, src_hbm.ap()[t * 128:(t + 1) * 128, :],
                        dst_slots=(s,))
                dma(em, 'dsy', OUT.ap()[t * 128:(t + 1) * 128, :], ap, src_slots=(s,))


def _acc_out(nc, em, P, ACC, acc_slot, OUT):
    with nc.Block() as blk:
        @blk.sync
        def _(sync):
            for t in range(NT):
                dma(em, 'dsy', OUT.ap()[t * 128:(t + 1) * 128, :],
                    ACC[:, t * 1024:(t + 1) * 1024], src_slots=(acc_slot[t],))


def _rstd(em, VE, AC, stat, col_var, col_out, scratch_cols, eps=EPS, newton=2):
    c1, c2 = scratch_cols
    v = stat[:, col_var:col_var + 1]
    s1 = stat[:, c1:c1 + 1]
    s2 = stat[:, c2:c2 + 1]
    out = stat[:, col_out:col_out + 1]
    em.prod(VE.tensor_scalar(s1, v, float(eps), None, ALU.add), 've')
    em.after('ac', [em.tail('ve')])
    em.prod(AC.activation(s2, s1, AF.Sqrt), 'ac')
    em.after('ve', [em.tail('ac')])
    em.prod(VE.reciprocal(out, s2), 've')
    em.vesync()
    for _ in range(newton):
        em.prod(VE.tensor_tensor(s2, out, out, ALU.mult), 've')
        em.prod(VE.tensor_tensor(s2, s2, s1, ALU.mult), 've')
        em.prod(VE.tensor_scalar(s2, s2, -0.5, 1.5, ALU.mult, ALU.add), 've')
        em.prod(VE.tensor_tensor(out, out, s2, ALU.mult), 've')
    return em.tail('ve')


def _phase1(nc, em, P, IN, xt_slot, x1_hbm):
    PE, VE, AC, GP, SY = nc.tensor, nc.vector, nc.scalar, nc.gpsimd, nc.sync
    XT = P['XT']
    import contextlib
    es0 = contextlib.ExitStack()
    es1 = contextlib.ExitStack()
    es2 = contextlib.ExitStack()

    def sb(e, name, shape, dt):
        return e.enter_context(nc.sbuf_tensor(name, shape, dt))

    def ps(e, name, shape, dt):
        return e.enter_context(nc.psum_tensor(name, shape, dt))

    LOC = sb(es0, 'loc_tm', [128, NT * 1024], BF16)
    CMP = sb(es0, 'cmp_tm', [128, NT * 1024], BF16)
    Sps = ps(es0, 'S_ps', [128, 2048], F32)
    TRp = ps(es0, 'TR_ps', [128, 1024], BF16)
    Ops = ps(es0, 'O_ps', [128, 512], F32)
    Mp0 = ps(es0, 'M_ps', [128, 512], F32)
    Mp1 = ps(es0, 'M_ps2', [128, 512], F32)
    WQKV = sb(es1, 'wqkv', [128, 2 * 3072], BF16)
    Qt = sb(es1, 'qT', [128, 2048], BF16)
    Kt = sb(es1, 'kT', [128, 2048], BF16)
    Vt = sb(es1, 'vT', [128, 2048], BF16)
    Vp = sb(es1, 'Vp', [128, 2048], BF16)
    KC = sb(es1, 'kcompT', [128, 128], BF16)
    VC = sb(es1, 'vcomp', [128, 128], BF16)
    COSB = sb(es1, 'co_sb', [128, 128], BF16)
    COT = sb(es1, 'co_t', [128, 64], BF16)
    COT2 = sb(es1, 'co_t2', [128, 64], BF16)
    CST = sb(es1, 'cst', [128, 16], F32)
    Psb = sb(es1, 'P_sb', [128, 2048], BF16)
    PTsb = sb(es1, 'PT_sb', [128, 2048], BF16)
    FST = sb(es1, 'fl_st', [128, 16], F32)
    late = {}

    def alloc_late():
        es1.close()
        late['CPW'] = sb(es2, 'cprojw', [128, 8 * 1024], BF16)
        late['GWB'] = sb(es2, 'gwbuf', [128, 2 * 1024], BF16)
        late['GSML'] = sb(es2, 'gsml', [128, 48], BF16)
        late['GROW'] = sb(es2, 'grow', [1, 1152], F32)
        late['XST'] = sb(es2, 'xstage', [128, 2 * 1024], F32)
        late['F32A'] = sb(es2, 'f32a', [128, 1024], F32)
        late['F32B'] = sb(es2, 'f32b', [128, 1024], F32)
        late['F32C'] = sb(es2, 'f32c', [128, 1024], F32)
        late['SCRB'] = sb(es2, 'scr_bf', [128, 1024], BF16)
        late['QXB'] = sb(es2, 'qx_bf', [128, 1024], BF16)
        late['GBC'] = sb(es2, 'gbc', [128, 4], F32)

    with nc.Block() as blk:
        @blk.sync
        def _(sync):
            if _DBG:
                for _b in (COSB, COT, COT2, CST, KC, VC, LOC, CMP, Qt, Kt, Vp):
                    em.prod(VE.memset(_b[:], 0.0), 've')
                for _e in ('pe', 'ac', 'gp', 'sy'):
                    em.after(_e, [em.tail('ve')])
            wslot = [Slot(), Slot()]
            sl_q, sl_k, sl_v, sl_vp = Slot(), Slot(), Slot(), Slot()
            sl_kc, sl_vc = [Slot(), Slot()], [Slot(), Slot()]
            sl_cosb, sl_cot, sl_cot2 = Slot(), Slot(), Slot()
            sl_S, sl_TR, sl_O = Slot(), Slot(), Slot()
            sl_M = [Slot(), Slot()]
            sl_P, sl_PT = Slot(), Slot()
            sl_loc = [Slot() for _ in range(NT)]
            sl_cmp = [Slot() for _ in range(NT)]
            sl_cst, sl_fst = Slot(), Slot()
            Mp = [Mp0, Mp1]

            def fetch_pair(hp):
                b = WQKV[:, (hp % 2) * 3072:(hp % 2) * 3072 + 3072]
                ws = wslot[hp % 2]
                ws.wr(em, 'gp')
                for j, base in enumerate((0, 1024, 2048)):
                    dma(em, 'dgp', b[:, j * 1024:(j + 1) * 1024],
                        bass.AP(IN['c_attn_w'], base + hp * 128,
                                [[3072, 128], [3072 * 128, 8], [1, 128]]))
                ws.set(em.qtail('dgp'))
                return b

            fetch_pair(0)

            for hp in range(8):
                wbuf = WQKV[:, (hp % 2) * 3072:(hp % 2) * 3072 + 3072]
                ws = wslot[hp % 2]
                if hp < 7:
                    fetch_pair(hp + 1)

                for (dst, sl, wofs, bcol) in ((Qt, sl_q, 0, hp), (Kt, sl_k, 1024, 8 + hp),
                                              (Vt, sl_v, 2048, 16 + hp)):
                    for n in range(4):
                        mp = Mp[n % 2]
                        ms = sl_M[n % 2]
                        ms.wr(em, 'pe')
                        ws.rd(em, 'pe')
                        for tt in range(4 * n, 4 * n + 4):
                            xt_slot[tt].rd(em, 'pe')
                        for dc in range(ND):
                            inst = MM(em, mp[:], wbuf[:, wofs + dc * 128:wofs + (dc + 1) * 128],
                                             XT[:, dc * T + n * 512:dc * T + (n + 1) * 512],
                                             start=(dc == 0), stop=(dc == ND - 1))
                        dep = em.prod(inst, 'pe')
                        ms.set(dep)
                        ws.used(dep)
                        for tt in range(4 * n, 4 * n + 4):
                            xt_slot[tt].used(dep)
                        sl.wr(em, 'ac')
                        ms.rd(em, 'ac')
                        i2 = AC.activation(dst[:, n * 512:(n + 1) * 512], mp[:], AF.Identity,
                                           bias=P['cab_s'][:, bcol:bcol + 1], scale=1.0)
                        d2 = em.prod(i2, 'ac')
                        ms.used(d2)
                        sl.users.append(d2)
                    sl.set(em.tail('ac'))

                for t in range(NT):
                    mp = Mp[t % 2]
                    ms = sl_M[t % 2]
                    ms.wr(em, 'pe')
                    ws.rd(em, 'pe')
                    xt_slot[t].rd(em, 'pe')
                    for dc in range(ND):
                        inst = MM(em, mp[:, 0:128],
                                         XT[:, dc * T + t * 128:dc * T + (t + 1) * 128],
                                         wbuf[:, 2048 + dc * 128:2048 + (dc + 1) * 128],
                                         start=(dc == 0), stop=(dc == ND - 1))
                    dep = em.prod(inst, 'pe')
                    ms.set(dep)
                    ws.used(dep)
                    xt_slot[t].used(dep)
                    sl_vp.wr(em, 've')
                    ms.rd(em, 've')
                    i2 = VE.tensor_tensor(Vp[:, t * 128:(t + 1) * 128], mp[:, 0:128],
                                          P['vb_bc'][:, hp * 128:(hp + 1) * 128], ALU.add)
                    d2 = em.prod(i2, 've')
                    ms.used(d2)
                    sl_vp.users.append(d2)
                sl_vp.set(em.tail('ve'))

                for hl in range(2):
                    r0 = hl * 64
                    for (srcT, srcsl, is_k) in ((Kt, sl_k, True), (Vt, sl_v, False)):
                        mp, ms = Mp1, sl_M[1]
                        ms.wr(em, 'pe')
                        srcsl.rd(em, 'pe')
                        for kk in range(32):
                            inst = MM(em, mp[r0:r0 + 64, 0:127],
                                             P['convw_s'][r0:r0 + 64, kk * 64:(kk + 1) * 64],
                                             srcT[r0:r0 + 64, kk:kk + 16 * 126 + 1:16],
                                             start=(kk == 0), stop=(kk == 31))
                        dep = em.prod(inst, 'pe')
                        ms.set(dep)
                        srcsl.used(dep)
                        sl_cosb.wr(em, 'ac')
                        ms.rd(em, 'ac')
                        i2 = AC.activation(COSB[r0:r0 + 64, 0:127], mp[r0:r0 + 64, 0:127],
                                           AF.Identity, bias=P['convb_s'][r0:r0 + 64, :], scale=1.0)
                        d2 = em.prod(i2, 'ac')
                        ms.used(d2)
                        sl_cosb.set(d2)
                        sl_TR.wr(em, 'pe')
                        sl_cosb.rd(em, 'pe')
                        i3 = TR(em, TRp[0:127, 0:64], COSB[r0:r0 + 64, 0:127],
                                          P['idb'][r0:r0 + 64, r0:r0 + 64])
                        d3 = em.prod(i3, 'pe')
                        sl_TR.set(d3)
                        sl_cosb.used(d3)
                        sl_cot.wr(em, 've')
                        sl_TR.rd(em, 've')
                        i4 = VE.tensor_copy(COT[0:127, :], TRp[0:127, 0:64])
                        d4 = em.prod(i4, 've')
                        sl_TR.used(d4)
                        sl_cot.set(d4)
                        sl_cst.wr(em, 've')
                        em.prod(VE.bn_stats(CST[0:127, 0:6], COT[0:127, :]), 've')
                        em.vesync()
                        em.prod(VE.bn_aggr(CST[0:127, 6:8], CST[0:127, 0:6]), 've')
                        em.vesync()
                        _rstd(em, VE, AC, CST[0:127, :], 7, 8, (9, 10))
                        em.prod(VE.tensor_tensor(CST[0:127, 9:10], CST[0:127, 6:7],
                                                 CST[0:127, 8:9], ALU.mult), 've')
                        em.prod(VE.tensor_scalar(CST[0:127, 9:10], CST[0:127, 9:10], -1.0,
                                                 None, ALU.mult), 've')
                        sl_cst.set(em.tail('ve'))
                        em.after('ac', [em.tail('ve')])
                        sl_cot2.wr(em, 'ac')
                        i5 = AC.activation(COT2[0:127, :], COT[0:127, :], AF.Identity,
                                           bias=CST[0:127, 9:10], scale=CST[0:127, 8:9])
                        em.prod(i5, 'ac')
                        em.after('ve', [em.tail('ac')])
                        em.prod(VE.tensor_tensor(COT2[0:127, :], COT2[0:127, :],
                                                 P['compwb_bc'][0:127, 0:64], ALU.mult), 've')
                        em.prod(VE.tensor_tensor(COT2[0:127, :], COT2[0:127, :],
                                                 P['compwb_bc'][0:127, 64:128], ALU.add), 've')
                        em.after('ac', [em.tail('ve')])
                        if _DBG and hp == 0 and hl == 0 and is_k:
                            dma(em, 'dgp', IN['dbg_co'].ap(), COSB[:], src_slots=(sl_cosb,))
                            dma(em, 'dgp', IN['dbg_cot'].ap(), COT[:], src_slots=(sl_cot,))
                            dma(em, 'dgp', IN['dbg_cot2'].ap(), COT2[:], src_slots=(sl_cot2,))
                            dma(em, 'dsy', IN['dbg_cst'].ap(), CST[:], src_slots=(sl_cst,))
                        dst_sl = sl_kc[hl] if is_k else sl_vc[hl]
                        if is_k:
                            i6 = AC.activation(COT[0:127, :], COT2[0:127, :], AF.Gelu)
                            d6 = em.prod(i6, 'ac')
                            sl_cot.set(d6)
                            sl_TR.wr(em, 'pe')
                            sl_cot.rd(em, 'pe')
                            i7 = TR(em, TRp[r0:r0 + 64, 0:127], COT[0:127, :],
                                              P['idb'][0:127, 0:127])
                            d7 = em.prod(i7, 'pe')
                            sl_TR.set(d7)
                            sl_cot.used(d7)
                            dst_sl.wr(em, 've')
                            sl_TR.rd(em, 've')
                            i8 = VE.tensor_copy(KC[r0:r0 + 64, 0:127], TRp[r0:r0 + 64, 0:127])
                            d8 = em.prod(i8, 've')
                            sl_TR.used(d8)
                            dst_sl.set(d8)
                        else:
                            dst_sl.wr(em, 'ac')
                            i6 = AC.activation(VC[0:127, r0:r0 + 64], COT2[0:127, :], AF.Gelu)
                            d6 = em.prod(i6, 'ac')
                            dst_sl.set(d6)
                        sl_cot2.set(em.tail('ac'))

                if _DBG and hp == 0:
                    for nm, buf, sl in (('dbg_q', Qt, sl_q), ('dbg_k', Kt, sl_k),
                                        ('dbg_vp', Vp, sl_vp), ('dbg_kc', KC, sl_kc[0]),
                                        ('dbg_vc', VC, sl_vc[0])):
                        dma(em, 'dgp', IN[nm].ap(), buf[:], src_slots=(sl,))
                for hl in range(2):
                    r0 = hl * 64
                    h = hp * 2 + hl
                    for qi in range(NT):
                        vis = (qi + 1) * 128
                        nch = (vis + 511) // 512
                        sl_S.wr(em, 'pe')
                        sl_q.rd(em, 'pe')
                        sl_k.rd(em, 'pe')
                        for ch in range(nch):
                            w_ = min(512, vis - ch * 512)
                            inst = MM(em, Sps[:, ch * 512:ch * 512 + w_],
                                             Qt[r0:r0 + 64, qi * 128:(qi + 1) * 128],
                                             Kt[r0:r0 + 64, ch * 512:ch * 512 + w_],
                                             start=True, stop=True)
                        depS = em.prod(inst, 'pe')
                        sl_S.set(depS)
                        sl_q.used(depS)
                        sl_k.used(depS)
                        sl_S.rd(em, 've')
                        i2 = VE.tensor_tensor(Sps[:, vis - 128:vis], Sps[:, vis - 128:vis],
                                              P['mtri'][:], ALU.add)
                        dmask = em.prod(i2, 've')
                        sl_S.set(dmask)
                        sl_fst.wr(em, 've')
                        em.prod(VE.tensor_reduce(FST[:, 0:1], Sps[:, 0:vis], AX.X, ALU.max), 've')
                        em.prod(VE.tensor_scalar(FST[:, 1:2], FST[:, 0:1], -0.125, None,
                                                 ALU.mult), 've')
                        sl_fst.set(em.tail('ve'))
                        em.after('ac', [em.tail('ve')])
                        sl_P.wr(em, 'ac')
                        i3 = AC.activation(Psb[:, 0:vis], Sps[:, 0:vis], AF.Exp,
                                           bias=FST[:, 1:2], scale=0.125,
                                           accum_out=FST[:, 2:3])
                        d3 = em.prod(i3, 'ac')
                        sl_P.set(d3)
                        sl_S.used(d3)
                        sl_fst.users.append(d3)
                        em.after('ve', [d3])
                        em.prod(VE.reciprocal(FST[:, 3:4], FST[:, 2:3]), 've')
                        em.vesync()
                        sl_fst.set(em.tail('ve'))
                        nks = vis // 128
                        for g0 in range(0, nks, 8):
                            gn = min(8, nks - g0)
                            sl_TR.wr(em, 'pe')
                            sl_P.rd(em, 'pe')
                            for j in range(gn):
                                ks = g0 + j
                                it = TR(em, TRp[:, j * 128:(j + 1) * 128],
                                                  Psb[:, ks * 128:(ks + 1) * 128], P['idb'][:])
                            dt = em.prod(it, 'pe')
                            sl_TR.set(dt)
                            sl_P.used(dt)
                            sl_PT.wr(em, 've')
                            sl_TR.rd(em, 've')
                            ic = VE.tensor_copy(PTsb[:, g0 * 128:(g0 + gn) * 128],
                                                TRp[:, 0:gn * 128])
                            dc_ = em.prod(ic, 've')
                            sl_TR.used(dc_)
                            sl_PT.set(dc_)
                        sl_O.wr(em, 'pe')
                        sl_PT.rd(em, 'pe')
                        sl_vp.rd(em, 'pe')
                        for ks in range(nks):
                            io = MM(em, Ops[:, 0:64], PTsb[:, ks * 128:(ks + 1) * 128],
                                           Vp[:, ks * 128 + r0:ks * 128 + r0 + 64],
                                           start=(ks == 0), stop=(ks == nks - 1))
                        do = em.prod(io, 'pe')
                        sl_O.set(do)
                        sl_PT.used(do)
                        sl_vp.used(do)
                        em.after('ac', [em.tail('ve')])
                        sl_O.rd(em, 'ac')
                        sl_loc[qi].wr(em, 'ac')
                        i4 = AC.activation(LOC[:, qi * 1024 + h * 64:qi * 1024 + h * 64 + 64],
                                           Ops[:, 0:64], AF.Identity, scale=FST[:, 3:4])
                        d4 = em.prod(i4, 'ac')
                        sl_O.used(d4)
                        sl_fst.users.append(d4)
                        sl_loc[qi].users.append(d4)

                        sl_S.wr(em, 'pe')
                        sl_q.rd(em, 'pe')
                        sl_kc[hl].rd(em, 'pe')
                        i5 = MM(em, Sps[:, 0:127], Qt[r0:r0 + 64, qi * 128:(qi + 1) * 128],
                                       KC[r0:r0 + 64, 0:127], start=True, stop=True)
                        d5 = em.prod(i5, 'pe')
                        sl_S.set(d5)
                        sl_q.used(d5)
                        sl_kc[hl].used(d5)
                        if qi == 0:
                            sl_S.rd(em, 've')
                            i6 = VE.tensor_tensor(Sps[:, 0:127], Sps[:, 0:127],
                                                  P['mcomp'][:, 0:127], ALU.add)
                            em.prod(i6, 've')
                            sl_S.set(em.tail('ve'))
                        sl_fst.wr(em, 've')
                        sl_S.rd(em, 've')
                        em.prod(VE.tensor_reduce(FST[:, 4:5], Sps[:, 0:127], AX.X, ALU.max), 've')
                        em.prod(VE.tensor_scalar(FST[:, 5:6], FST[:, 4:5], -0.125, None,
                                                 ALU.mult), 've')
                        sl_fst.set(em.tail('ve'))
                        em.after('ac', [em.tail('ve')])
                        sl_P.wr(em, 'ac')
                        i7 = AC.activation(Psb[:, 0:127], Sps[:, 0:127], AF.Exp,
                                           bias=FST[:, 5:6], scale=0.125, accum_out=FST[:, 6:7])
                        d7 = em.prod(i7, 'ac')
                        sl_P.set(d7)
                        sl_S.used(d7)
                        sl_fst.users.append(d7)
                        em.after('ve', [d7])
                        em.prod(VE.reciprocal(FST[:, 7:8], FST[:, 6:7]), 've')
                        em.vesync()
                        sl_fst.set(em.tail('ve'))
                        sl_TR.wr(em, 'pe')
                        sl_P.rd(em, 'pe')
                        i8 = TR(em, TRp[0:127, 0:128], Psb[:, 0:127], P['idb'][:])
                        d8 = em.prod(i8, 'pe')
                        sl_TR.set(d8)
                        sl_P.used(d8)
                        sl_PT.wr(em, 've')
                        sl_TR.rd(em, 've')
                        i9 = VE.tensor_copy(PTsb[0:127, 0:128], TRp[0:127, 0:128])
                        d9 = em.prod(i9, 've')
                        sl_TR.used(d9)
                        sl_PT.set(d9)
                        sl_O.wr(em, 'pe')
                        sl_PT.rd(em, 'pe')
                        sl_vc[hl].rd(em, 'pe')
                        i10 = MM(em, Ops[:, 64:128], PTsb[0:127, 0:128],
                                        VC[0:127, r0:r0 + 64], start=True, stop=True)
                        d10 = em.prod(i10, 'pe')
                        sl_O.set(d10)
                        sl_PT.used(d10)
                        sl_vc[hl].used(d10)
                        em.after('ac', [em.tail('ve')])
                        sl_O.rd(em, 'ac')
                        sl_cmp[qi].wr(em, 'ac')
                        i11 = AC.activation(CMP[:, qi * 1024 + h * 64:qi * 1024 + h * 64 + 64],
                                            Ops[:, 64:128], AF.Identity, scale=FST[:, 7:8])
                        d11 = em.prod(i11, 'ac')
                        sl_O.used(d11)
                        sl_fst.users.append(d11)
                        sl_cmp[qi].users.append(d11)
            for t in range(NT):
                sl_loc[t].set(em.tail('ac'))
                sl_cmp[t].set(em.tail('ac'))

            if _DBG:
                dma(em, 'dgp', IN['dbg_loc'].ap(), LOC[:], src_slots=(sl_loc[0],))
                dma(em, 'dgp', IN['dbg_cmp'].ap(), CMP[:], src_slots=(sl_cmp[0],))
            tails = [em.tail(s) for s in ('pe', 've', 'ac', 'gp')]
            for q_ in ('dsy', 'dgp', 'dsc'):
                tails += em.qtail(q_)
            for e_ in ('pe', 've', 'ac', 'gp', 'sy'):
                em.after(e_, tails)
            alloc_late()
            CPW, GWB, GSML, GROW, XST = (late['CPW'], late['GWB'], late['GSML'],
                                         late['GROW'], late['XST'])
            F32A, F32B, F32C = late['F32A'], late['F32B'], late['F32C']
            SCRB, QXB, GBC = late['SCRB'], late['QXB'], late['GBC']
            cpw_slot = Slot()
            dma(em, 'dgp', CPW[:],
                bass.AP(IN['c_proj_w'], 0, [[1024, 128], [1024 * 128, 8], [1, 1024]]),
                dst_slots=(cpw_slot,))

            sl_scrb = Slot()
            sl_stat = Slot()
            for t in range(NT):
                tile = LOC[:, t * 1024:(t + 1) * 1024]
                sl_loc[t].rd(em, 've')
                sl_stat.wr(em, 've')
                em.prod(VE.bn_stats(P['STAT'][:, 0:6], tile[:, 0:512]), 've')
                em.vesync()
                em.prod(VE.bn_stats(P['STAT'][:, 6:12], tile[:, 512:1024]), 've')
                em.vesync()
                em.prod(VE.bn_aggr(P['STAT'][:, 12:14], P['STAT'][:, 0:12]), 've')
                em.vesync()
                _rstd(em, VE, AC, P['STAT'], 13, 14, (15, 16))
                em.prod(VE.tensor_tensor(P['STAT'][:, 15:16], P['STAT'][:, 12:13],
                                         P['STAT'][:, 14:15], ALU.mult), 've')
                em.prod(VE.tensor_scalar(P['STAT'][:, 15:16], P['STAT'][:, 15:16], -1.0,
                                         None, ALU.mult), 've')
                sl_stat.set(em.tail('ve'))
                em.after('ac', [em.tail('ve')])
                sl_scrb.wr(em, 'ac')
                i1 = AC.activation(SCRB[:], tile, AF.Identity,
                                   bias=P['STAT'][:, 15:16], scale=P['STAT'][:, 14:15])
                d1 = em.prod(i1, 'ac')
                sl_scrb.set(d1)
                sl_stat.users.append(d1)
                sl_loc[t].users.append(d1)
                em.after('ve', [d1])
                sl_loc[t].wr(em, 've')
                em.prod(VE.tensor_tensor(SCRB[:], SCRB[:], P['lnl_bc'][:, 0:1024], ALU.mult), 've')
                i2 = VE.tensor_tensor(tile, SCRB[:], P['lnl_bc'][:, 1024:2048], ALU.add)
                d2 = em.prod(i2, 've')
                sl_scrb.users.append(d2)
                sl_loc[t].set(d2)

            sl_gsml = Slot()
            for half, (src_, ssl) in enumerate(((LOC, sl_loc[15]), (CMP, sl_cmp[15]))):
                sl_TR.wr(em, 'pe')
                ssl.rd(em, 'pe')
                for dc in range(8):
                    it = TR(em, TRp[:, dc * 128:(dc + 1) * 128],
                                      src_[:, 15 * 1024 + dc * 128:15 * 1024 + (dc + 1) * 128],
                                      P['idb'][:])
                dt = em.prod(it, 'pe')
                sl_TR.set(dt)
                ssl.used(dt)
                em.after('ve', [dt])
                sl_gsml.wr(em, 've')
                ig = VE.tensor_copy(GSML[:, half * 8:half * 8 + 8], TRp[:, 127:1024:128])
                dg = em.prod(ig, 've')
                sl_TR.used(dg)
            sl_gsml.set(em.tail('ve'))

            sl_gw = [Slot(), Slot()]

            def fetch_gw1(j):
                b = GWB[:, (j % 2) * 1024:(j % 2) * 1024 + 1024]
                s = sl_gw[j % 2]
                s.wr(em, 'gp')
                dma(em, 'dgp', b, bass.AP(IN['gw1'], j * 128 * 1024,
                                          [[1024, 128], [1, 1024]]))
                s.set(em.qtail('dgp'))

            fetch_gw1(0)
            fetch_gw1(1)
            sl_S.wr(em, 'pe')
            for j in range(16):
                sl_gw[j % 2].rd(em, 'pe')
                sl_gsml.rd(em, 'pe')
                for half in range(2):
                    im = MM(em, Sps[0:1, half * 512:(half + 1) * 512],
                                   GSML[:, j:j + 1],
                                   GWB[:, (j % 2) * 1024 + half * 512:(j % 2) * 1024 + (half + 1) * 512],
                                   start=(j == 0), stop=(j == 15), skip_group_check=True)
                dmm = em.prod(im, 'pe')
                sl_gw[j % 2].used(dmm)
                sl_gsml.used(dmm)
                if j + 2 < 16:
                    fetch_gw1(j + 2)
            sl_S.set(em.tail('pe'))
            rowsl = Slot()
            rowsl.wr(em, 'sy')
            dgb1 = dma(em, 'dsy', P['ROWS'][0:1, 0:1024], IN['gb1'].ap())
            rowsl.set(dgb1)
            em.after('ve', [dgb1, em.tail('pe')])
            iv = VE.tensor_tensor(GROW[0:1, 0:1024], Sps[0:1, 0:1024],
                                  P['ROWS'][0:1, 0:1024], ALU.add)
            em.prod(iv, 've')
            rowsl.used(em.tail('ve'))
            sl_S.used(em.tail('ve'))
            em.prod(VE.bn_stats(GROW[0:1, 1024:1030], GROW[0:1, 0:512]), 've')
            em.vesync()
            em.prod(VE.bn_stats(GROW[0:1, 1030:1036], GROW[0:1, 512:1024]), 've')
            em.vesync()
            em.prod(VE.bn_aggr(GROW[0:1, 1036:1038], GROW[0:1, 1024:1036]), 've')
            em.vesync()
            _rstd(em, VE, AC, GROW[0:1, :], 1037, 1038, (1039, 1040))
            em.prod(VE.tensor_tensor(GROW[0:1, 1039:1040], GROW[0:1, 1036:1037],
                                     GROW[0:1, 1038:1039], ALU.mult), 've')
            em.prod(VE.tensor_scalar(GROW[0:1, 1039:1040], GROW[0:1, 1039:1040], -1.0,
                                     None, ALU.mult), 've')
            em.after('ac', [em.tail('ve')])
            em.prod(AC.activation(GROW[0:1, 0:1024], GROW[0:1, 0:1024], AF.Identity,
                                  bias=GROW[0:1, 1039:1040], scale=GROW[0:1, 1038:1039]), 'ac')
            rowsl.wr(em, 'sy')
            dglnw = dma(em, 'dsy', P['ROWS'][0:1, 0:1024], IN['glnw'].ap())
            rowsl.set(dglnw)
            em.after('ve', [em.tail('ac'), dglnw])
            em.prod(VE.tensor_tensor(GROW[0:1, 0:1024], GROW[0:1, 0:1024],
                                     P['ROWS'][0:1, 0:1024], ALU.mult), 've')
            rowsl.used(em.tail('ve'))
            rowsl.wr(em, 'sy')
            dglnb = dma(em, 'dsy', P['ROWS'][0:1, 0:1024], IN['glnb'].ap())
            rowsl.set(dglnb)
            em.after('ve', [dglnb])
            em.prod(VE.tensor_tensor(GROW[0:1, 0:1024], GROW[0:1, 0:1024],
                                     P['ROWS'][0:1, 0:1024], ALU.add), 've')
            rowsl.used(em.tail('ve'))
            em.prod(VE.tensor_scalar(GROW[0:1, 0:1024], GROW[0:1, 0:1024], 0.0, None,
                                     ALU.max), 've')
            sl_M[0].wr(em, 'pe')
            em.after('pe', [em.tail('ve')])
            for j in range(8):
                it = TR(em, Mp0[:, 20 + j:21 + j], GROW[0:1, j * 128:(j + 1) * 128],
                                  P['idf'][0:1, 0:1])
            sl_M[0].set(em.prod(it, 'pe'))
            em.after('ve', [em.tail('pe')])
            em.prod(VE.tensor_copy(GSML[:, 17:25], Mp0[:, 20:28]), 've')
            sl_M[0].used(em.tail('ve'))
            gw2_slot = Slot()
            dma(em, 'dgp', GSML[:, 25:41],
                bass.AP(IN['gw2'], 0, [[2, 128], [2 * 128, 8], [1, 2]]),
                dst_slots=(gw2_slot,))
            sl_M[0].wr(em, 'pe')
            gw2_slot.rd(em, 'pe')
            em.after('pe', [em.tail('ve')])
            for dc in range(8):
                im = MM(em, Mp0[0:1, 0:2], GSML[:, 17 + dc:18 + dc],
                               GSML[:, 25 + dc * 2:25 + dc * 2 + 2],
                               start=(dc == 0), stop=(dc == 7))
            dmm = em.prod(im, 'pe')
            sl_M[0].set(dmm)
            gw2_slot.used(dmm)
            rowsl.wr(em, 'sy')
            dgb2 = dma(em, 'dsy', P['ROWS'][0:1, 0:2], IN['gb2'].ap())
            rowsl.set(dgb2)
            em.after('ve', [dgb2, dmm])
            em.prod(VE.tensor_tensor(GROW[0:1, 0:2], Mp0[0:1, 0:2], P['ROWS'][0:1, 0:2],
                                     ALU.add), 've')
            rowsl.used(em.tail('ve'))
            sl_M[0].used(em.tail('ve'))
            em.prod(VE.tensor_reduce(GROW[0:1, 4:5], GROW[0:1, 0:2], AX.X, ALU.max), 've')
            em.prod(VE.tensor_scalar(GROW[0:1, 5:6], GROW[0:1, 4:5], -1.0, None, ALU.mult), 've')
            em.after('ac', [em.tail('ve')])
            em.prod(AC.activation(GROW[0:1, 0:2], GROW[0:1, 0:2], AF.Exp,
                                  bias=GROW[0:1, 5:6], scale=1.0, accum_out=GROW[0:1, 6:7]), 'ac')
            em.after('ve', [em.tail('ac')])
            em.prod(VE.reciprocal(GROW[0:1, 7:8], GROW[0:1, 6:7]), 've')
            em.vesync()
            em.prod(VE.tensor_scalar(GROW[0:1, 0:2], GROW[0:1, 0:2], GROW[0:1, 7:8], None,
                                     ALU.mult), 've')
            if _DBG:
                dma(em, 'dsy', IN['dbg_g'].ap()[0:1, 0:2], GROW[0:1, 0:2])
            sl_M[0].wr(em, 'pe')
            em.after('pe', [em.tail('ve')])
            im = MM(em, Mp0[:, 0:2], P['onesf'][:], GROW[0:1, 0:2], start=True, stop=True)
            dmm = em.prod(im, 'pe')
            sl_M[0].set(dmm)
            em.after('ve', [dmm])
            gbc_slot = Slot()
            em.prod(VE.tensor_copy(GBC[:, 0:2], Mp0[:, 0:2]), 've')
            sl_M[0].used(em.tail('ve'))
            gbc_slot.set(em.tail('ve'))

            for t in range(NT):
                tile = LOC[:, t * 1024:(t + 1) * 1024]
                ctile = CMP[:, t * 1024:(t + 1) * 1024]
                sl_scrb.wr(em, 've')
                sl_cmp[t].rd(em, 've')
                gbc_slot.rd(em, 've')
                em.prod(VE.tensor_scalar(SCRB[:], ctile, GBC[:, 1:2], None, ALU.mult), 've')
                sl_cmp[t].used(em.tail('ve'))
                sl_scrb.set(em.tail('ve'))
                sl_loc[t].rd(em, 've')
                sl_loc[t].wr(em, 've')
                iv = VE.scalar_tensor_tensor(tile, tile, GBC[:, 0:1], SCRB[:],
                                             ALU.mult, ALU.add)
                dv = em.prod(iv, 've')
                gbc_slot.used(dv)
                sl_scrb.users.append(dv)
                sl_loc[t].set(dv)
                sl_TR.wr(em, 'pe')
                sl_loc[t].rd(em, 'pe')
                for dc in range(8):
                    it = TR(em, TRp[:, dc * 128:(dc + 1) * 128],
                                      tile[:, dc * 128:(dc + 1) * 128], P['idb'][:])
                dt = em.prod(it, 'pe')
                sl_TR.set(dt)
                sl_loc[t].used(dt)
                xt_slot[t].wr(em, 've')
                sl_TR.rd(em, 've')
                ic = VE.tensor_copy(
                    bass.AP(XT, t * 128, [[XT.shape[1], 128], [T, 8], [1, 128]]),
                    TRp[:, 0:1024])
                dc_ = em.prod(ic, 've')
                sl_TR.used(dc_)
                xt_slot[t].set(dc_)

            sl_xst = [Slot(), Slot()]
            sl_f32 = Slot()
            sl_qxb = Slot()
            for t in range(NT):
                tile_x = XST[:, (t % 2) * 1024:(t % 2) * 1024 + 1024]
                dma(em, 'dsy', tile_x, IN['x_tm'].ap()[t * 128:(t + 1) * 128, :],
                    dst_slots=(sl_xst[t % 2],))
                for half in range(2):
                    mp, ms = Mp[half], sl_M[half]
                    ms.wr(em, 'pe')
                    cpw_slot.rd(em, 'pe')
                    xt_slot[t].rd(em, 'pe')
                    for dc in range(8):
                        im = MM(em, mp[:],
                                       XT[:, dc * T + t * 128:dc * T + (t + 1) * 128],
                                       CPW[:, dc * 1024 + half * 512:dc * 1024 + (half + 1) * 512],
                                       start=(dc == 0), stop=(dc == 7))
                    dmm = em.prod(im, 'pe')
                    ms.set(dmm)
                    cpw_slot.used(dmm)
                    xt_slot[t].used(dmm)
                sl_f32.wr(em, 've')
                sl_xst[t % 2].rd(em, 've')
                sl_M[0].rd(em, 've')
                sl_M[1].rd(em, 've')
                em.prod(VE.tensor_tensor(F32A[:, 0:512], tile_x[:, 0:512], Mp0[:], ALU.add), 've')
                em.prod(VE.tensor_tensor(F32A[:, 512:1024], tile_x[:, 512:1024], Mp1[:],
                                         ALU.add), 've')
                dv = em.tail('ve')
                sl_M[0].used(dv)
                sl_M[1].used(dv)
                sl_xst[t % 2].used(dv)
                em.prod(VE.tensor_tensor(F32A[:], F32A[:], P['cpb_bc'][:], ALU.add), 've')
                sl_f32.set(em.tail('ve'))
                sl_stat.wr(em, 've')
                em.prod(VE.bn_stats(P['STAT'][:, 0:6], F32A[:, 0:512]), 've')
                em.vesync()
                em.prod(VE.bn_stats(P['STAT'][:, 6:12], F32A[:, 512:1024]), 've')
                em.vesync()
                em.prod(VE.bn_aggr(P['STAT'][:, 12:14], P['STAT'][:, 0:12]), 've')
                em.vesync()
                _rstd(em, VE, AC, P['STAT'], 13, 14, (15, 16))
                em.prod(VE.tensor_tensor(P['STAT'][:, 15:16], P['STAT'][:, 12:13],
                                         P['STAT'][:, 14:15], ALU.mult), 've')
                em.prod(VE.tensor_scalar(P['STAT'][:, 15:16], P['STAT'][:, 15:16], -1.0,
                                         None, ALU.mult), 've')
                sl_stat.set(em.tail('ve'))
                em.after('ac', [em.tail('ve')])
                i1 = AC.activation(F32B[:], F32A[:], AF.Identity,
                                   bias=P['STAT'][:, 15:16], scale=P['STAT'][:, 14:15])
                d1 = em.prod(i1, 'ac')
                sl_stat.users.append(d1)
                sl_f32.users.append(d1)
                em.after('sy', [d1])
                dx1 = dma(em, 'dsy', x1_hbm.ap()[t * 128:(t + 1) * 128, :], F32B[:])
                em.x1_deps = getattr(em, 'x1_deps', []) + [dx1]
                em.after('ve', [d1])
                em.prod(VE.tensor_reduce(P['STAT'][:, 20:21], F32B[:], AX.X, ALU.max,
                                         apply_absolute_value=True), 've')
                em.prod(VE.tensor_scalar(P['STAT'][:, 21:22], P['STAT'][:, 20:21], 1e-5,
                                         None, ALU.max), 've')
                em.prod(VE.reciprocal(P['STAT'][:, 22:23], P['STAT'][:, 21:22]), 've')
                em.vesync()
                em.prod(VE.tensor_scalar(P['sx'][:, t:t + 1], P['STAT'][:, 22:23], 127.0,
                                         None, ALU.mult), 've')
                em.prod(VE.reciprocal(P['isx'][:, t:t + 1], P['sx'][:, t:t + 1]), 've')
                em.vesync()
                dsc = em.tail('ve')
                em.after('ac', [dsc])
                i2 = AC.activation(F32A[:], F32B[:], AF.Identity,
                                   bias=P['STAT'][:, 511:512], scale=P['sx'][:, t:t + 1])
                d2 = em.prod(i2, 'ac')
                sl_f32.set(d2)
                em.after('ve', [d2, dx1])
                em.prod(VE.tensor_copy(F32C[:], F32A[:]), 've')
                for B in (128.0, 64.0, 32.0, 16.0, 8.0, 4.0, 2.0, 1.0):
                    em.prod(VE.tensor_scalar(F32B[:], F32A[:], B, B, ALU.is_ge, ALU.mult), 've')
                    em.prod(VE.tensor_tensor(F32A[:], F32A[:], F32B[:], ALU.subtract), 've')
                em.prod(VE.tensor_tensor(F32C[:], F32C[:], F32A[:], ALU.subtract), 've')
                sl_qxb.wr(em, 've')
                iq = VE.tensor_scalar(QXB[:], F32C[:], -128.0, None, ALU.add)
                dq = em.prod(iq, 've')
                sl_qxb.set(dq)
                sl_f32.set(dq)
                sl_M[0].wr(em, 'pe')
                em.after('pe', [dsc])
                it = TR(em, Mp0[0:1, 0:128], P['isx'][:, t:t + 1], P['idf'][:])
                dt = em.prod(it, 'pe')
                sl_M[0].set(dt)
                em.after('ve', [dt])
                em.prod(VE.tensor_copy(P['invsxrow'][0:1, t * 128:(t + 1) * 128],
                                       Mp0[0:1, 0:128]), 've')
                sl_M[0].used(em.tail('ve'))
                sl_TR.wr(em, 'pe')
                sl_qxb.rd(em, 'pe')
                for dc in range(8):
                    it = TR(em, TRp[:, dc * 128:(dc + 1) * 128],
                                      QXB[:, dc * 128:(dc + 1) * 128], P['idb'][:])
                dt = em.prod(it, 'pe')
                sl_TR.set(dt)
                sl_qxb.used(dt)
                xt_slot[t].wr(em, 've')
                sl_TR.rd(em, 've')
                ic = VE.tensor_copy(
                    bass.AP(XT, t * 128, [[XT.shape[1], 128], [T, 8], [1, 128]]),
                    TRp[:, 0:1024])
                dc_ = em.prod(ic, 've')
                sl_TR.used(dc_)
                xt_slot[t].set(dc_)


    es2.close()
    es0.close()


def _phase2(nc, em, P, IN, xt_slot, x1_hbm, x2_hbm, w1q_hbm, w2q_hbm):
    PE, VE, AC, GP, SY = nc.tensor, nc.vector, nc.scalar, nc.gpsimd, nc.sync
    XT = P['XT']
    import contextlib

    def sb(e, name, shape, dt):
        return e.enter_context(nc.sbuf_tensor(name, shape, dt))

    def ps(e, name, shape, dt):
        return e.enter_context(nc.psum_tensor(name, shape, dt))

    es0 = contextlib.ExitStack()
    MROW = sb(es0, 'p2_mrow', [1, 16], F32)
    ST2 = sb(es0, 'p2_st2', [128, 16], F32)
    PA0 = ps(es0, 'p2_pa0', [128, 512], F32)
    PA1 = ps(es0, 'p2_pa1', [128, 512], F32)
    PB = [ps(es0, 'p2_pb%d' % i, [128, 512], F32) for i in range(4)]
    TRB = ps(es0, 'p2_trb', [128, 1024], BF16)
    PBC = ps(es0, 'p2_pbc', [128, 512], F32)

    esA = contextlib.ExitStack()
    WQS = sb(esA, 'p2_wqs', [128, 4096], F32)
    QF1 = sb(esA, 'p2_qf1', [128, 2048], F32)
    QF2 = sb(esA, 'p2_qf2', [128, 2048], F32)
    QOUT = sb(esA, 'p2_qout', [128, 4096], BF16)
    with nc.Block() as blk:
        @blk.sync
        def _(sync):
            em.prod(VE.memset(ST2[:, 0:1], 1.0), 've')
            em.prod(VE.memset(ST2[:, 1:2], 0.0), 've')
            sl_stg, sl_q1, sl_q2, sl_qo = Slot(), Slot(), Slot(), Slot()
            sl_st2, sl_mrow, sl_pbc = Slot(), Slot(), Slot()
            for wi, (wname, nchunk, cw) in enumerate((('w1T', 8, 4096), ('w2T', 32, 1024))):
                em.prod(VE.memset(ST2[:, 1:2], 0.0), 've')
                for c in range(nchunk):
                    dma(em, 'dsy', WQS[:, 0:cw],
                        IN[wname].ap()[c * 128:(c + 1) * 128, :], dst_slots=(sl_stg,))
                    sl_stg.rd(em, 've')
                    em.prod(VE.tensor_reduce(ST2[:, 2:3], WQS[:, 0:cw], AX.X, ALU.add,
                                             apply_absolute_value=True), 've')
                    em.prod(VE.tensor_tensor(ST2[:, 1:2], ST2[:, 1:2], ST2[:, 2:3],
                                             ALU.add), 've')
                    sl_stg.used(em.tail('ve'))
                sl_pbc.wr(em, 'pe')
                em.after('pe', [em.tail('ve')])
                im = MM(em, PBC[0:1, 0:1], ST2[:, 0:1], ST2[:, 1:2], start=True, stop=True)
                dmm = em.prod(im, 'pe')
                sl_pbc.set(dmm)
                em.after('ve', [dmm])
                mc = 4 * wi
                em.prod(VE.tensor_scalar(MROW[0:1, mc:mc + 1], PBC[0:1, 0:1],
                                         1.0 / 4194304.0, None, ALU.mult), 've')
                sl_pbc.used(em.tail('ve'))
                em.prod(VE.tensor_scalar(MROW[0:1, mc:mc + 1], MROW[0:1, mc:mc + 1],
                                         1e-5, None, ALU.max), 've')
                em.prod(VE.reciprocal(MROW[0:1, mc + 1:mc + 2], MROW[0:1, mc:mc + 1]), 've')
                sl_mrow.set(em.tail('ve'))
                for j, col in ((1, 4 + wi), (0, 6 + wi)):
                    sl_pbc.wr(em, 'pe')
                    em.after('pe', [em.tail('ve')])
                    im = MM(em, PBC[:, 0:1], P['onesf'][:], MROW[0:1, mc + j:mc + j + 1],
                            start=True, stop=True)
                    dmm = em.prod(im, 'pe')
                    sl_pbc.set(dmm)
                    em.after('ve', [dmm])
                    sl_st2.wr(em, 've')
                    em.prod(VE.tensor_copy(ST2[:, col:col + 1], PBC[:, 0:1]), 've')
                    sl_pbc.used(em.tail('ve'))
                    sl_st2.set(em.tail('ve'))
                for c in range(nchunk):
                    dma(em, 'dsy', WQS[:, 0:cw],
                        IN[wname].ap()[c * 128:(c + 1) * 128, :], dst_slots=(sl_stg,))
                    sl_stg.rd(em, 've')
                    sl_st2.rd(em, 've')
                    for hh in range(cw // 2048) or [0]:
                        pass
                    nh = max(1, cw // 2048)
                    for hh in range(nh):
                        wdt = min(2048, cw)
                        sl_q1.wr(em, 've')
                        sl_q2.wr(em, 've')
                        em.prod(VE.tensor_scalar(QF1[:, 0:wdt], WQS[:, hh * 2048:hh * 2048 + wdt],
                                                 ST2[:, 4 + wi:5 + wi], None, ALU.mult), 've')
                        em.prod(VE.tensor_scalar(QF2[:, 0:wdt], QF1[:, 0:wdt], 0.5, None,
                                                 ALU.is_gt), 've')
                        em.prod(VE.tensor_scalar(QF1[:, 0:wdt], QF1[:, 0:wdt], -0.5, None,
                                                 ALU.is_lt), 've')
                        sl_qo.wr(em, 've')
                        em.prod(VE.tensor_tensor(QOUT[:, hh * 2048:hh * 2048 + wdt],
                                                 QF2[:, 0:wdt], QF1[:, 0:wdt],
                                                 ALU.subtract), 've')
                        sl_q1.set(em.tail('ve'))
                        sl_q2.set(em.tail('ve'))
                    sl_stg.used(em.tail('ve'))
                    sl_qo.set(em.tail('ve'))
                    if wi == 0:
                        outap = bass.AP(w1q_hbm, c * 16384, [[128, 128], [131072, 32], [1, 128]])
                    else:
                        outap = bass.AP(w2q_hbm, c * 131072, [[1024, 128], [1, 1024]])
                    dma(em, 'dsy', outap, QOUT[:, 0:cw], src_slots=(sl_qo,))
    esA.close()

    esB = contextlib.ExitStack()
    W1S = sb(esB, 'p2_w1s', [128, 2 * 1024], BF16)
    W2S = sb(esB, 'p2_w2s', [128, 2 * 1024], BF16)
    HS = sb(esB, 'p2_hs', [128, 32 * 258], BF16)
    GS = sb(esB, 'p2_gs', [128, 32 * 256], BF16)
    GT = sb(esB, 'p2_gt', [128, 2 * 4096], BF16)
    QS = sb(esB, 'p2_qs', [128, 32 * 256], BF16)
    F32A = sb(esB, 'p2_fa', [128, 2048], F32)
    F32B = sb(esB, 'p2_fb', [128, 2048], F32)
    F32C = sb(esB, 'p2_fc', [128, 2048], F32)
    XSTG = sb(esB, 'p2_xstg', [128, 2 * 1024], F32)
    SBC = sb(esB, 'p2_sbc', [128, 516], F32)
    CONVF = sb(esB, 'p2_convf', [128, 96], F32)
    CONVB = sb(esB, 'p2_convb', [128, 32], F32)
    T1 = sb(esB, 'p2_t1', [128, 256], BF16)
    T2 = sb(esB, 'p2_t2', [128, 256], BF16)
    HALO = sb(esB, 'p2_halo', [128, 32], BF16)
    ST3 = sb(esB, 'p2_st3', [128, 8], F32)
    with nc.Block() as blk:
        @blk.sync
        def _(sync):
            em.after('sy', getattr(em, 'x1_deps', []))
            dma(em, 'dsy', CONVF[:], bass.AP(IN['fconvw'], 0, [[3, 128], [384, 32], [1, 3]]),
                slow=True)
            dcb = dma(em, 'dsy', CONVB[:], bass.AP(IN['fconvb'], 0, [[1, 128], [128, 32]]),
                      slow=True)
            em.prod(VE.memset(HALO[:], 0.0), 've')
            sl_w1 = [Slot(), Slot()]
            sl_w2 = [Slot(), Slot()]
            sl_hs, sl_gs, sl_gt, sl_qs = Slot(), Slot(), Slot(), Slot()
            sl_fa, sl_fb, sl_fc = Slot(), Slot(), Slot()
            sl_pa = [Slot(), Slot()]
            sl_pb = [Slot() for _ in range(4)]
            sl_trb, sl_pbc = Slot(), Slot()
            sl_sbc, sl_st3, sl_rows = Slot(), Slot(), Slot()
            sl_t1, sl_t2, sl_halo = Slot(), Slot(), Slot()
            sl_xstg = [Slot(), Slot()]
            sl_mrow = Slot()
            sl_rowscr = Slot()
            PAb = [PA0, PA1]
            sl_halo.set(em.tail('ve'))

            for s in range(8):
                t0tok = s * 256
                wcols = min(257, T - t0tok)
                tiles = sorted(set([t0tok // 128, (t0tok + wcols - 1) // 128]))
                sl_rows.wr(em, 've')
                sl_mrow.rd(em, 've')
                em.prod(VE.tensor_scalar(P['ROWS'][0:1, 0:wcols],
                                         P['invsxrow'][0:1, t0tok:t0tok + wcols],
                                         MROW[0:1, 0:1], None, ALU.mult), 've')
                sl_rows.set(em.tail('ve'))
                sl_pbc.wr(em, 'pe')
                sl_rows.rd(em, 'pe')
                im = MM(em, PBC[:, 0:wcols], P['onesf'][:], P['ROWS'][0:1, 0:wcols],
                        start=True, stop=True)
                dmm = em.prod(im, 'pe')
                sl_pbc.set(dmm)
                sl_rows.used(dmm)
                em.after('ve', [dmm])
                sl_sbc.wr(em, 've')
                em.prod(VE.tensor_copy(SBC[:, 256:256 + wcols], PBC[:, 0:wcols]), 've')
                sl_pbc.used(em.tail('ve'))
                sl_sbc.set(em.tail('ve'))
                sl_hs.wr(em, 've')
                sl_halo.rd(em, 've')
                em.prod(VE.tensor_copy(
                    bass.AP(HS, 0, [[HS.shape[1], 128], [258, 32], [1, 1]]),
                    HALO[:]), 've')
                if s == 7:
                    em.prod(VE.memset(
                        bass.AP(HS, 257, [[HS.shape[1], 128], [258, 32], [1, 1]]), 0.0), 've')
                sl_halo.used(em.tail('ve'))
                sl_hs.set(em.tail('ve'))
                for hc in range(32):
                    b = W1S[:, (hc % 2) * 1024:(hc % 2) * 1024 + 1024]
                    dma(em, 'dgp', b,
                        bass.AP(w1q_hbm, hc * 131072, [[128, 128], [16384, 8], [1, 128]]),
                        dst_slots=(sl_w1[hc % 2],))
                    mp, msl = PAb[hc % 2], sl_pa[hc % 2]
                    msl.wr(em, 'pe')
                    sl_w1[hc % 2].rd(em, 'pe')
                    for tt in tiles:
                        xt_slot[tt].rd(em, 'pe')
                    for dc in range(8):
                        im = MM(em, mp[:, 0:wcols], b[:, dc * 128:(dc + 1) * 128],
                                XT[:, dc * T + t0tok:dc * T + t0tok + wcols],
                                start=(dc == 0), stop=(dc == 7))
                    dmm = em.prod(im, 'pe')
                    msl.set(dmm)
                    sl_w1[hc % 2].used(dmm)
                    for tt in tiles:
                        xt_slot[tt].used(dmm)
                    em.after('ve', [dmm])
                    sl_sbc.rd(em, 've')
                    iv = VE.tensor_tensor(HS[:, hc * 258 + 1:hc * 258 + 1 + wcols],
                                          mp[:, 0:wcols], SBC[:, 256:256 + wcols], ALU.mult)
                    dv = em.prod(iv, 've')
                    msl.used(dv)
                    sl_sbc.used(dv)
                    sl_hs.users.append(dv)
                sl_hs.set(em.tail('ve'))
                if s < 7:
                    sl_halo.wr(em, 've')
                    em.prod(VE.tensor_copy(
                        HALO[:],
                        bass.AP(HS, 256, [[HS.shape[1], 128], [258, 32], [1, 1]])), 've')
                    sl_hs.users.append(em.tail('ve'))
                    sl_halo.set(em.tail('ve'))
                gw = min(256, T - t0tok)
                for hc in range(32):
                    sl_t1.wr(em, 'ac')
                    sl_hs.rd(em, 'ac')
                    dcbw = dcb
                    em.after('ac', [dcbw])
                    i1 = AC.activation(T1[:, 0:gw], HS[:, hc * 258 + 1:hc * 258 + 1 + gw],
                                       AF.Identity, bias=CONVB[:, hc:hc + 1],
                                       scale=CONVF[:, hc * 3 + 1:hc * 3 + 2])
                    d1 = em.prod(i1, 'ac')
                    sl_t1.set(d1)
                    sl_hs.users.append(d1)
                    em.after('ve', [d1])
                    sl_t2.wr(em, 've')
                    i2 = VE.scalar_tensor_tensor(T2[:, 0:gw], HS[:, hc * 258:hc * 258 + gw],
                                                 CONVF[:, hc * 3:hc * 3 + 1], T1[:, 0:gw],
                                                 ALU.mult, ALU.add)
                    d2 = em.prod(i2, 've')
                    sl_t1.used(d2)
                    sl_t2.set(d2)
                    sl_t1.wr(em, 've')
                    i3 = VE.scalar_tensor_tensor(T1[:, 0:gw], HS[:, hc * 258 + 2:hc * 258 + 2 + gw],
                                                 CONVF[:, hc * 3 + 2:hc * 3 + 3], T2[:, 0:gw],
                                                 ALU.mult, ALU.add)
                    d3 = em.prod(i3, 've')
                    sl_t2.used(d3)
                    sl_t1.set(d3)
                    sl_hs.users.append(d3)
                    em.after('ac', [d3])
                    sl_gs.wr(em, 'ac')
                    i4 = AC.activation(GS[:, hc * 256:hc * 256 + gw], T1[:, 0:gw], AF.Silu)
                    d4 = em.prod(i4, 'ac')
                    sl_t1.used(d4)
                    sl_gs.users.append(d4)
                sl_gs.set(em.tail('ac'))
                for tt in range(2):
                    for g8 in range(4):
                        sl_trb.wr(em, 'pe')
                        sl_gs.rd(em, 'pe')
                        for j in range(8):
                            hc = g8 * 8 + j
                            it = TR(em, TRB[:, j * 128:(j + 1) * 128],
                                    GS[:, hc * 256 + tt * 128:hc * 256 + tt * 128 + 128],
                                    P['idb'][:])
                        dt = em.prod(it, 'pe')
                        sl_trb.set(dt)
                        sl_gs.used(dt)
                        em.after('ve', [dt])
                        sl_gt.wr(em, 've')
                        ic = VE.tensor_copy(GT[:, tt * 4096 + g8 * 1024:tt * 4096 + (g8 + 1) * 1024],
                                            TRB[:, 0:1024])
                        dc_ = em.prod(ic, 've')
                        sl_trb.used(dc_)
                        sl_gt.users.append(dc_)
                sl_gt.set(em.tail('ve'))
                for tt in range(2):
                    sl_gt.rd(em, 've')
                    sl_st3.wr(em, 've')
                    em.prod(VE.tensor_reduce(ST3[:, 4:5], GT[:, tt * 4096:(tt + 1) * 4096],
                                             AX.X, ALU.max, apply_absolute_value=True), 've')
                    em.vesync()
                    em.prod(VE.tensor_scalar(ST3[:, 4:5], ST3[:, 4:5], 1e-5, None, ALU.max), 've')
                    em.prod(VE.reciprocal(ST3[:, 5:6], ST3[:, 4:5]), 've')
                    em.prod(VE.tensor_scalar(ST3[:, tt:tt + 1], ST3[:, 5:6], 127.0, None,
                                             ALU.mult), 've')
                    em.prod(VE.reciprocal(ST3[:, 6:7], ST3[:, tt:tt + 1]), 've')
                    sl_mrow.rd(em, 've')
                    em.prod(VE.tensor_scalar(ST3[:, 2 + tt:3 + tt], ST3[:, 6:7],
                                             ST2[:, 7:8], None, ALU.mult), 've')
                    sl_gt.used(em.tail('ve'))
                    sl_st3.set(em.tail('ve'))
                    dma(em, 'dsy', bass.AP(nc._row_scr, tt * 128, [[1, 128], [1, 1]]),
                        ST3[:, tt:tt + 1], src_slots=(sl_st3,), dst_slots=(sl_rowscr,))
                sl_rows.wr(em, 'sy')
                drow = dma(em, 'dsy', P['ROWS'][0:1, 0:256],
                           bass.AP(nc._row_scr, 0, [[256, 1], [1, 256]]),
                           src_slots=(sl_rowscr,))
                sl_rows.set(drow)
                sl_pbc.wr(em, 'pe')
                sl_rows.rd(em, 'pe')
                im = MM(em, PBC[:, 0:256], P['onesf'][:], P['ROWS'][0:1, 0:256],
                        start=True, stop=True)
                dmm = em.prod(im, 'pe')
                sl_pbc.set(dmm)
                sl_rows.used(dmm)
                em.after('ve', [dmm])
                sl_sbc.wr(em, 've')
                em.prod(VE.tensor_copy(SBC[:, 0:256], PBC[:, 0:256]), 've')
                sl_pbc.used(em.tail('ve'))
                sl_sbc.set(em.tail('ve'))
                for q4 in range(4):
                    sl_fa.wr(em, 've')
                    sl_gs.rd(em, 've')
                    sl_sbc.rd(em, 've')
                    ia = VE.tensor_tensor(
                        F32A[:].rearrange('p (a b) -> p a b', a=8),
                        bass.AP(GS, q4 * 8 * 256, [[GS.shape[1], 128], [256, 8], [1, 256]]),
                        bass.AP(SBC, 0, [[SBC.shape[1], 128], [0, 8], [1, 256]]),
                        ALU.mult)
                    da = em.prod(ia, 've')
                    sl_gs.used(da)
                    sl_sbc.used(da)
                    em.prod(VE.tensor_scalar(F32A[:], F32A[:], 128.5, None, ALU.add), 've')
                    sl_fb.wr(em, 've')
                    em.prod(VE.tensor_copy(F32B[:], F32A[:]), 've')
                    sl_fc.wr(em, 've')
                    for B in (128.0, 64.0, 32.0, 16.0, 8.0, 4.0, 2.0, 1.0):
                        em.prod(VE.tensor_scalar(F32C[:], F32A[:], B, B, ALU.is_ge,
                                                 ALU.mult), 've')
                        em.prod(VE.tensor_tensor(F32A[:], F32A[:], F32C[:],
                                                 ALU.subtract), 've')
                    em.prod(VE.tensor_tensor(F32B[:], F32B[:], F32A[:], ALU.subtract), 've')
                    sl_qs.wr(em, 've')
                    iq = VE.tensor_scalar(QS[:, q4 * 2048:(q4 + 1) * 2048], F32B[:],
                                          -128.0, None, ALU.add)
                    dq = em.prod(iq, 've')
                    sl_fa.set(dq)
                    sl_fb.set(dq)
                    sl_fc.set(dq)
                    sl_qs.users.append(dq)
                sl_qs.set(em.tail('ve'))
                for i in range(4):
                    sl_pb[i].wr(em, 'pe')
                for hc in range(32):
                    b = W2S[:, (hc % 2) * 1024:(hc % 2) * 1024 + 1024]
                    dma(em, 'dgp', b, bass.AP(w2q_hbm, hc * 131072, [[1024, 128], [1, 1024]]),
                        dst_slots=(sl_w2[hc % 2],))
                    sl_w2[hc % 2].rd(em, 'pe')
                    sl_qs.rd(em, 'pe')
                    for tt in range(2):
                        for half in range(2):
                            im = MM(em, PB[tt * 2 + half][:],
                                    QS[:, hc * 256 + tt * 128:hc * 256 + tt * 128 + 128],
                                    b[:, half * 512:(half + 1) * 512],
                                    start=(hc == 0), stop=(hc == 31),
                                    skip_group_check=True)
                    dmm = em.prod(im, 'pe')
                    sl_w2[hc % 2].used(dmm)
                    sl_qs.used(dmm)
                dpe = em.tail('pe')
                for i in range(4):
                    sl_pb[i].set(dpe)
                for tt in range(2):
                    gt_ = s * 2 + tt
                    stg = XSTG[:, tt * 1024:(tt + 1) * 1024]
                    dma(em, 'dsy', stg, x1_hbm.ap()[gt_ * 128:(gt_ + 1) * 128, :],
                        dst_slots=(sl_xstg[tt],))
                    em.after('ac', [dpe, sl_st3.ready[0] if sl_st3.ready else None] if False else [dpe])
                    sl_st3.rd(em, 'ac')
                    sl_fa.wr(em, 'ac')
                    for half in range(2):
                        i1 = AC.activation(F32A[:, half * 512:(half + 1) * 512],
                                           PB[tt * 2 + half][:], AF.Identity,
                                           scale=ST3[:, 2 + tt:3 + tt])
                        d1 = em.prod(i1, 'ac')
                        sl_pb[tt * 2 + half].used(d1)
                        sl_st3.used(d1)
                    sl_fa.set(em.tail('ac'))
                    em.after('ve', [em.tail('ac')])
                    sl_xstg[tt].rd(em, 've')
                    iv = VE.tensor_tensor(F32A[:, 0:1024], F32A[:, 0:1024], stg, ALU.add)
                    dv = em.prod(iv, 've')
                    sl_xstg[tt].used(dv)
                    sl_fa.set(dv)
                    sl_st3.wr(em, 've')
                    em.prod(VE.bn_stats(P['STAT'][:, 0:6], F32A[:, 0:512]), 've')
                    em.vesync()
                    em.prod(VE.bn_stats(P['STAT'][:, 6:12], F32A[:, 512:1024]), 've')
                    em.vesync()
                    em.prod(VE.bn_aggr(P['STAT'][:, 12:14], P['STAT'][:, 0:12]), 've')
                    em.vesync()
                    _rstd(em, VE, AC, P['STAT'], 13, 14, (15, 16))
                    em.prod(VE.tensor_tensor(P['STAT'][:, 15:16], P['STAT'][:, 12:13],
                                             P['STAT'][:, 14:15], ALU.mult), 've')
                    em.prod(VE.tensor_scalar(P['STAT'][:, 15:16], P['STAT'][:, 15:16],
                                             -1.0, None, ALU.mult), 've')
                    em.after('ac', [em.tail('ve')])
                    sl_fb.wr(em, 'ac')
                    i2 = AC.activation(F32B[:, 0:1024], F32A[:, 0:1024], AF.Identity,
                                       bias=P['STAT'][:, 15:16], scale=P['STAT'][:, 14:15])
                    d2 = em.prod(i2, 'ac')
                    sl_fa.users.append(d2)
                    sl_fb.set(d2)
                    dx2 = dma(em, 'dsy', x2_hbm.ap()[gt_ * 128:(gt_ + 1) * 128, :],
                              F32B[:, 0:1024], src_slots=(sl_fb,))
                    em.x2_deps = getattr(em, 'x2_deps', []) + [dx2]
                    sl_t1.wr(em, 've')
                    sl_t2.wr(em, 've')
                    em.after('ve', [d2])
                    em.prod(VE.tensor_copy(T1[:, 0:256], F32B[:, 0:256]), 've') if False else None
                    sl_qs.wr(em, 've')
                    ixb = VE.tensor_copy(QS[:, 0:1024], F32B[:, 0:1024])
                    dxb = em.prod(ixb, 've')
                    sl_fb.users.append(dxb)
                    sl_qs.set(dxb)
                    sl_trb.wr(em, 'pe')
                    sl_qs.rd(em, 'pe')
                    for dc in range(8):
                        it = TR(em, TRB[:, dc * 128:(dc + 1) * 128],
                                QS[:, dc * 128:(dc + 1) * 128], P['idb'][:])
                    dt = em.prod(it, 'pe')
                    sl_trb.set(dt)
                    sl_qs.used(dt)
                    xt_slot[gt_].wr(em, 've')
                    sl_trb.rd(em, 've')
                    ic = VE.tensor_copy(
                        bass.AP(XT, gt_ * 128, [[XT.shape[1], 128], [T, 8], [1, 128]]),
                        TRB[:, 0:1024])
                    dc_ = em.prod(ic, 've')
                    sl_trb.used(dc_)
                    xt_slot[gt_].set(dc_)
    esB.close()
    es0.close()


def _phase3(nc, em, P, IN, xt_slot, x2_hbm, x3_hbm):
    PE, VE, AC, GP, SY = nc.tensor, nc.vector, nc.scalar, nc.gpsimd, nc.sync
    XT = P['XT']
    import contextlib
    es = contextlib.ExitStack()

    def sb(name, shape, dt):
        return es.enter_context(nc.sbuf_tensor(name, shape, dt))

    def ps(name, shape, dt):
        return es.enter_context(nc.psum_tensor(name, shape, dt))

    OTM = sb('p3_otm', [128, NT * 1024], BF16)
    LATQ = sb('p3_latq', [128, 2048], BF16)
    LATK = sb('p3_latk', [128, 2048], BF16)
    LATV = sb('p3_latv', [128, 2048], BF16)
    QT = sb('p3_qt', [128, 2048], BF16)
    KT = sb('p3_kt', [128, 2048], BF16)
    VP = sb('p3_vp', [128, 2048], BF16)
    TMP = sb('p3_tmp', [128, 512], BF16)
    TMP2 = sb('p3_tmp2', [128, 512], BF16)
    PSB = sb('p3_psb', [128, 2048], BF16)
    PTSB = sb('p3_ptsb', [128, 2048], BF16)
    WDN = sb('p3_wdn', [128, 2 * 3072], BF16)
    WBLK = sb('p3_wblk', [128, 3 * 128], BF16)
    RBT = sb('p3_rbt', [128, 128], BF16)
    COSF = sb('p3_cosf', [128, 2048], BF16)
    SINF = sb('p3_sinf', [128, 2048], BF16)
    WCS = sb('p3_wcs', [128, 8 * 1024], BF16)
    XSTG = sb('p3_xstg', [128, 2 * 1024], F32)
    XB = sb('p3_xb', [128, 1024], BF16)
    FST = sb('p3_fst', [128, 16], F32)
    Sps = ps('p3_sps', [128, 2048], F32)
    TRp = ps('p3_trp', [128, 1024], BF16)
    Ops = ps('p3_ops', [128, 512], F32)
    Mp0 = ps('p3_mp0', [128, 512], F32)
    Mp1 = ps('p3_mp1', [128, 512], F32)
    with nc.Block() as blk:
        @blk.sync
        def _(sync):
            Mp = [Mp0, Mp1]
            sl_M = [Slot(), Slot()]
            wslot = [Slot(), Slot()]
            sl_lq, sl_lk, sl_lv = Slot(), Slot(), Slot()
            sl_qt, sl_kt, sl_vp = Slot(), Slot(), Slot()
            sl_tmp, sl_tmp2 = Slot(), Slot()
            sl_S, sl_TR, sl_O, sl_P, sl_PT = Slot(), Slot(), Slot(), Slot(), Slot()
            sl_otm = [Slot() for _ in range(NT)]
            sl_fst = Slot()
            sl_wc, sl_wblk = Slot(), Slot()
            sl_xstg = [Slot(), Slot()]
            sl_xb = Slot()
            sl_stat = Slot()

            for o, i in ((RBT[:], IN['RbigT'].ap()), (COSF[:], IN['cosF'].ap()),
                         (SINF[:], IN['sinF'].ap())):
                dma(em, 'dgp', o, i)
            dma(em, 'dgp', WBLK[:, 0:128], IN['Wq_blk'].ap())
            dma(em, 'dgp', WBLK[:, 128:256], IN['Wk_blk'].ap())
            dcst = dma(em, 'dgp', WBLK[:, 256:384], IN['Wv_blk'].ap())
            sl_wblk.set(em.qtail('dgp'))
            dma(em, 'dgp', WCS[:],
                bass.AP(IN['Wc'], 0, [[1024, 128], [1024 * 128, 8], [1, 1024]]),
                dst_slots=(sl_wc,))

            def fetch_pair(hp):
                b = WDN[:, (hp % 2) * 3072:(hp % 2) * 3072 + 3072]
                ws = wslot[hp % 2]
                ws.wr(em, 'gp')
                for j, wn in enumerate(('Wq_down', 'Wk_down', 'Wv_down')):
                    dma(em, 'dgp', b[:, j * 1024:(j + 1) * 1024],
                        bass.AP(IN[wn], hp * 128,
                                [[1024, 128], [1024 * 128, 8], [1, 128]]))
                ws.set(em.qtail('dgp'))
                return b

            fetch_pair(0)
            for hp in range(8):
                wbuf = WDN[:, (hp % 2) * 3072:(hp % 2) * 3072 + 3072]
                ws = wslot[hp % 2]
                if hp < 7:
                    fetch_pair(hp + 1)
                for (dst, sl, wofs) in ((LATQ, sl_lq, 0), (LATK, sl_lk, 1024),
                                        (LATV, sl_lv, 2048)):
                    for n in range(4):
                        mp, ms = Mp[n % 2], sl_M[n % 2]
                        ms.wr(em, 'pe')
                        ws.rd(em, 'pe')
                        for tt in range(4 * n, 4 * n + 4):
                            xt_slot[tt].rd(em, 'pe')
                        for dc in range(ND):
                            inst = MM(em, mp[:], wbuf[:, wofs + dc * 128:wofs + (dc + 1) * 128],
                                      XT[:, dc * T + n * 512:dc * T + (n + 1) * 512],
                                      start=(dc == 0), stop=(dc == ND - 1))
                        dep = em.prod(inst, 'pe')
                        ms.set(dep)
                        ws.used(dep)
                        for tt in range(4 * n, 4 * n + 4):
                            xt_slot[tt].used(dep)
                        sl.wr(em, 've')
                        ms.rd(em, 've')
                        i2 = VE.tensor_copy(dst[:, n * 512:(n + 1) * 512], mp[:])
                        d2 = em.prod(i2, 've')
                        ms.used(d2)
                        sl.users.append(d2)
                    sl.set(em.tail('ve'))
                for (lat, latsl, wcol, dstT, dstsl) in (
                        (LATQ, sl_lq, 0, QT, sl_qt), (LATK, sl_lk, 128, KT, sl_kt)):
                    for n in range(4):
                        mp, ms = Mp[0], sl_M[0]
                        ms.wr(em, 'pe')
                        latsl.rd(em, 'pe')
                        sl_wblk.rd(em, 'pe')
                        im = MM(em, mp[:], WBLK[:, wcol:wcol + 128],
                                lat[:, n * 512:(n + 1) * 512], start=True, stop=True)
                        dmm = em.prod(im, 'pe')
                        ms.set(dmm)
                        latsl.used(dmm)
                        sl_wblk.used(dmm)
                        em.after('ve', [dmm])
                        sl_tmp.wr(em, 've')
                        i2 = VE.tensor_copy(TMP[:], mp[:])
                        d2 = em.prod(i2, 've')
                        ms.used(d2)
                        sl_tmp.set(d2)
                        mp2, ms2 = Mp[1], sl_M[1]
                        ms2.wr(em, 'pe')
                        sl_tmp.rd(em, 'pe')
                        im = MM(em, mp2[:], RBT[:], TMP[:], start=True, stop=True)
                        dr = em.prod(im, 'pe')
                        ms2.set(dr)
                        sl_tmp.used(dr)
                        em.after('ve', [dr])
                        sl_tmp2.wr(em, 've')
                        i3 = VE.tensor_tensor(TMP2[:], mp2[:],
                                              SINF[:, n * 512:(n + 1) * 512], ALU.mult)
                        d3 = em.prod(i3, 've')
                        ms2.used(d3)
                        sl_tmp2.set(d3)
                        dstsl.wr(em, 've')
                        sl_tmp.rd(em, 've')
                        em.prod(VE.tensor_tensor(dstT[:, n * 512:(n + 1) * 512], TMP[:],
                                                 COSF[:, n * 512:(n + 1) * 512],
                                                 ALU.mult), 've')
                        i4 = VE.tensor_tensor(dstT[:, n * 512:(n + 1) * 512],
                                              dstT[:, n * 512:(n + 1) * 512],
                                              TMP2[:], ALU.add)
                        d4 = em.prod(i4, 've')
                        sl_tmp.used(d4)
                        sl_tmp2.used(d4)
                        dstsl.users.append(d4)
                    dstsl.set(em.tail('ve'))
                for t in range(NT):
                    mp, ms = Mp[t % 2], sl_M[t % 2]
                    ms.wr(em, 'pe')
                    sl_lv.rd(em, 'pe')
                    sl_wblk.rd(em, 'pe')
                    im = MM(em, mp[:, 0:128], LATV[:, t * 128:(t + 1) * 128],
                            WBLK[:, 256:384], start=True, stop=True)
                    dmm = em.prod(im, 'pe')
                    ms.set(dmm)
                    sl_lv.used(dmm)
                    sl_wblk.used(dmm)
                    sl_vp.wr(em, 've')
                    ms.rd(em, 've')
                    i2 = VE.tensor_copy(VP[:, t * 128:(t + 1) * 128], mp[:, 0:128])
                    d2 = em.prod(i2, 've')
                    ms.used(d2)
                    sl_vp.users.append(d2)
                sl_vp.set(em.tail('ve'))

                for hl in range(2):
                    r0 = hl * 64
                    h = hp * 2 + hl
                    for qi in range(NT):
                        vis = (qi + 1) * 128
                        nch = (vis + 511) // 512
                        sl_S.wr(em, 'pe')
                        sl_qt.rd(em, 'pe')
                        sl_kt.rd(em, 'pe')
                        for ch in range(nch):
                            w_ = min(512, vis - ch * 512)
                            inst = MM(em, Sps[:, ch * 512:ch * 512 + w_],
                                      QT[r0:r0 + 64, qi * 128:(qi + 1) * 128],
                                      KT[r0:r0 + 64, ch * 512:ch * 512 + w_],
                                      start=True, stop=True)
                        depS = em.prod(inst, 'pe')
                        sl_S.set(depS)
                        sl_qt.used(depS)
                        sl_kt.used(depS)
                        sl_S.rd(em, 've')
                        i2 = VE.tensor_tensor(Sps[:, vis - 128:vis], Sps[:, vis - 128:vis],
                                              P['mtri'][:], ALU.add)
                        em.prod(i2, 've')
                        sl_S.set(em.tail('ve'))
                        sl_fst.wr(em, 've')
                        em.prod(VE.tensor_reduce(FST[:, 0:1], Sps[:, 0:vis], AX.X,
                                                 ALU.max), 've')
                        em.vesync()
                        em.prod(VE.tensor_scalar(FST[:, 1:2], FST[:, 0:1], -0.125, None,
                                                 ALU.mult), 've')
                        sl_fst.set(em.tail('ve'))
                        em.after('ac', [em.tail('ve')])
                        sl_P.wr(em, 'ac')
                        i3 = AC.activation(PSB[:, 0:vis], Sps[:, 0:vis], AF.Exp,
                                           bias=FST[:, 1:2], scale=0.125,
                                           accum_out=FST[:, 2:3])
                        d3 = em.prod(i3, 'ac')
                        sl_P.set(d3)
                        sl_S.used(d3)
                        sl_fst.users.append(d3)
                        em.after('ve', [d3])
                        em.prod(VE.reciprocal(FST[:, 3:4], FST[:, 2:3]), 've')
                        em.vesync()
                        sl_fst.set(em.tail('ve'))
                        nks = vis // 128
                        for g0 in range(0, nks, 8):
                            gn = min(8, nks - g0)
                            sl_TR.wr(em, 'pe')
                            sl_P.rd(em, 'pe')
                            for j in range(gn):
                                ks = g0 + j
                                it = TR(em, TRp[:, j * 128:(j + 1) * 128],
                                        PSB[:, ks * 128:(ks + 1) * 128], P['idb'][:])
                            dt = em.prod(it, 'pe')
                            sl_TR.set(dt)
                            sl_P.used(dt)
                            sl_PT.wr(em, 've')
                            sl_TR.rd(em, 've')
                            ic = VE.tensor_copy(PTSB[:, g0 * 128:(g0 + gn) * 128],
                                                TRp[:, 0:gn * 128])
                            dc_ = em.prod(ic, 've')
                            sl_TR.used(dc_)
                            sl_PT.set(dc_)
                        sl_O.wr(em, 'pe')
                        sl_PT.rd(em, 'pe')
                        sl_vp.rd(em, 'pe')
                        for ks in range(nks):
                            io = MM(em, Ops[:, 0:64], PTSB[:, ks * 128:(ks + 1) * 128],
                                    VP[:, ks * 128 + r0:ks * 128 + r0 + 64],
                                    start=(ks == 0), stop=(ks == nks - 1))
                        do = em.prod(io, 'pe')
                        sl_O.set(do)
                        sl_PT.used(do)
                        sl_vp.used(do)
                        em.after('ac', [em.tail('ve')])
                        sl_O.rd(em, 'ac')
                        sl_otm[qi].wr(em, 'ac')
                        i4 = AC.activation(OTM[:, qi * 1024 + h * 64:qi * 1024 + h * 64 + 64],
                                           Ops[:, 0:64], AF.Identity, scale=FST[:, 3:4])
                        d4 = em.prod(i4, 'ac')
                        sl_O.used(d4)
                        sl_fst.users.append(d4)
                        sl_otm[qi].users.append(d4)
            for t in range(NT):
                sl_otm[t].set(em.tail('ac'))

            for t in range(NT):
                sl_TR.wr(em, 'pe')
                sl_otm[t].rd(em, 'pe')
                for dc in range(8):
                    it = TR(em, TRp[:, dc * 128:(dc + 1) * 128],
                            OTM[:, t * 1024 + dc * 128:t * 1024 + (dc + 1) * 128],
                            P['idb'][:])
                dt = em.prod(it, 'pe')
                sl_TR.set(dt)
                sl_otm[t].used(dt)
                xt_slot[t].wr(em, 've')
                sl_TR.rd(em, 've')
                ic = VE.tensor_copy(
                    bass.AP(XT, t * 128, [[XT.shape[1], 128], [T, 8], [1, 128]]),
                    TRp[:, 0:1024])
                dc_ = em.prod(ic, 've')
                sl_TR.used(dc_)
                xt_slot[t].set(dc_)

            em.after('sy', getattr(em, 'x2_deps', []))
            F32A = XSTG
            for t in range(NT):
                stg = XSTG[:, 0:1024]
                dma(em, 'dsy', stg, x2_hbm.ap()[t * 128:(t + 1) * 128, :],
                    dst_slots=(sl_xstg[0],))
                for half in range(2):
                    mp, ms = Mp[half], sl_M[half]
                    ms.wr(em, 'pe')
                    sl_wc.rd(em, 'pe')
                    xt_slot[t].rd(em, 'pe')
                    for dc in range(8):
                        im = MM(em, mp[:],
                                XT[:, dc * T + t * 128:dc * T + (t + 1) * 128],
                                WCS[:, dc * 1024 + half * 512:dc * 1024 + (half + 1) * 512],
                                start=(dc == 0), stop=(dc == 7))
                    dmm = em.prod(im, 'pe')
                    ms.set(dmm)
                    sl_wc.used(dmm)
                    xt_slot[t].used(dmm)
                work = XSTG[:, 1024:2048]
                sl_xstg[1].wr(em, 've')
                sl_xstg[0].rd(em, 've')
                sl_M[0].rd(em, 've')
                sl_M[1].rd(em, 've')
                em.prod(VE.tensor_tensor(work[:, 0:512], stg[:, 0:512], Mp0[:], ALU.add), 've')
                em.prod(VE.tensor_tensor(work[:, 512:1024], stg[:, 512:1024], Mp1[:],
                                         ALU.add), 've')
                dv = em.tail('ve')
                sl_M[0].used(dv)
                sl_M[1].used(dv)
                sl_xstg[0].used(dv)
                sl_xstg[1].set(dv)
                sl_stat.wr(em, 've')
                em.prod(VE.bn_stats(P['STAT'][:, 0:6], work[:, 0:512]), 've')
                em.vesync()
                em.prod(VE.bn_stats(P['STAT'][:, 6:12], work[:, 512:1024]), 've')
                em.vesync()
                em.prod(VE.bn_aggr(P['STAT'][:, 12:14], P['STAT'][:, 0:12]), 've')
                em.vesync()
                _rstd(em, VE, AC, P['STAT'], 13, 14, (15, 16))
                em.prod(VE.tensor_tensor(P['STAT'][:, 15:16], P['STAT'][:, 12:13],
                                         P['STAT'][:, 14:15], ALU.mult), 've')
                em.prod(VE.tensor_scalar(P['STAT'][:, 15:16], P['STAT'][:, 15:16], -1.0,
                                         None, ALU.mult), 've')
                sl_stat.set(em.tail('ve'))
                em.after('ac', [em.tail('ve')])
                sl_xstg[0].wr(em, 'ac')
                i1 = AC.activation(stg[:, 0:1024], work[:, 0:1024], AF.Identity,
                                   bias=P['STAT'][:, 15:16], scale=P['STAT'][:, 14:15])
                d1 = em.prod(i1, 'ac')
                sl_stat.users.append(d1)
                sl_xstg[1].users.append(d1)
                sl_xstg[0].set(d1)
                dx3 = dma(em, 'dsy', x3_hbm.ap()[t * 128:(t + 1) * 128, :], stg[:, 0:1024],
                          src_slots=(sl_xstg[0],))
                em.x3_deps = getattr(em, 'x3_deps', []) + [dx3]
                sl_xb.wr(em, 've')
                sl_xstg[0].rd(em, 've')
                ixb = VE.tensor_copy(XB[:], stg[:, 0:1024])
                dxb = em.prod(ixb, 've')
                sl_xstg[0].used(dxb)
                sl_xb.set(dxb)
                sl_TR.wr(em, 'pe')
                sl_xb.rd(em, 'pe')
                for dc in range(8):
                    it = TR(em, TRp[:, dc * 128:(dc + 1) * 128],
                            XB[:, dc * 128:(dc + 1) * 128], P['idb'][:])
                dt = em.prod(it, 'pe')
                sl_TR.set(dt)
                sl_xb.used(dt)
                xt_slot[t].wr(em, 've')
                sl_TR.rd(em, 've')
                ic = VE.tensor_copy(
                    bass.AP(XT, t * 128, [[XT.shape[1], 128], [T, 8], [1, 128]]),
                    TRp[:, 0:1024])
                dc_ = em.prod(ic, 've')
                sl_TR.used(dc_)
                xt_slot[t].set(dc_)
    es.close()


def _phase4(nc, em, P, IN, xt_slot, x3_hbm, OUT):
    PE, VE, AC, GP, SY = nc.tensor, nc.vector, nc.scalar, nc.gpsimd, nc.sync
    XT = P['XT']
    import contextlib
    es = contextlib.ExitStack()

    def sb(name, shape, dt):
        return es.enter_context(nc.sbuf_tensor(name, shape, dt))

    def ps(name, shape, dt):
        return es.enter_context(nc.psum_tensor(name, shape, dt))

    ACC = sb('p4_acc', [128, NT * 1024], F32)
    GSB = sb('p4_gsb', [128, 4 * 2048], BF16)
    USB = sb('p4_usb', [128, 4 * 2048], BF16)
    GB = sb('p4_gb', [128, 2 * 4096], BF16)
    UB = sb('p4_ub', [128, 2 * 4096], BF16)
    DB = sb('p4_db', [128, 4096], BF16)
    WD = sb('p4_wd', [128, 512], F32)
    SC = sb('p4_sc', [128, 512], F32)
    TMPW = sb('p4_tmpw', [128, 512], F32)
    TMP2 = sb('p4_tmp2', [128, 512], F32)
    TMP3 = sb('p4_tmp3', [128, 512], F32)
    EQS = sb('p4_eqs', [128, 4 * 512], BF16)
    W4 = sb('p4_w4', [128, 80], F32)
    RWT = sb('p4_rwt', [128, 8 * 32], F32)
    RBBC = sb('p4_rbbc', [128, 32], F32)
    LOGI = ps('p4_logi', [128, 512], F32)
    TRF = ps('p4_trf', [128, 512], F32)
    PG = [ps('p4_pg%d' % i, [128, 512], F32) for i in range(2)]
    PD = [ps('p4_pd%d' % i, [128, 512], F32) for i in range(2)]
    with nc.Block() as blk:
        @blk.sync
        def _(sync):
            sl_acc = [Slot() for _ in range(NT)]
            sl_logi, sl_trf = Slot(), Slot()
            sl_pg = [Slot(), Slot()]
            sl_pd = [Slot(), Slot()]
            sl_gsb, sl_usb = Slot(), Slot()
            sl_gb, sl_ub, sl_db = [Slot(), Slot()], [Slot(), Slot()], Slot()
            sl_wd, sl_sc, sl_tmpw, sl_tmp2 = Slot(), Slot(), Slot(), Slot()
            sl_eqs, sl_w4, sl_rbbc = Slot(), Slot(), Slot()
            sl_stat, sl_rows = Slot(), Slot()

            em.after('sy', getattr(em, 'x3_deps', []))
            for t in range(NT):
                dma(em, 'dsy', ACC[:, t * 1024:(t + 1) * 1024],
                    x3_hbm.ap()[t * 128:(t + 1) * 128, :], dst_slots=(sl_acc[t],))
            dma(em, 'dsy', RWT[:], bass.AP(IN['rwT'], 0, [[32, 128], [4096, 8], [1, 32]]))
            sl_rwt = Slot()
            sl_rwt.set(em.qtail('dsy'))

            esR = contextlib.ExitStack()
            XF32 = esR.enter_context(nc.sbuf_tensor('p4_xf32', [128, 2048], F32))
            sl_xf = Slot()
            first_mm = True
            for dc in range(8):
                for g4 in range(4):
                    sl_trf.wr(em, 'pe')
                    for j in range(4):
                        t = g4 * 4 + j
                        sl_acc[t].rd(em, 'pe')
                        it = TR(em, TRF[:, j * 128:(j + 1) * 128],
                                ACC[:, t * 1024 + dc * 128:t * 1024 + (dc + 1) * 128],
                                P['idf'][:])
                    dt = em.prod(it, 'pe')
                    sl_trf.set(dt)
                    for j in range(4):
                        sl_acc[g4 * 4 + j].used(dt)
                    em.after('ve', [dt])
                    sl_xf.wr(em, 've')
                    ic = VE.tensor_copy(XF32[:, g4 * 512:(g4 + 1) * 512], TRF[:, 0:512])
                    dc_ = em.prod(ic, 've')
                    sl_trf.used(dc_)
                    sl_xf.users.append(dc_)
                sl_xf.set(em.tail('ve'))
                sl_logi.wr(em, 'pe')
                sl_xf.rd(em, 'pe')
                sl_rwt.rd(em, 'pe')
                for t in range(NT):
                    im = MM(em, LOGI[:, t * 32:(t + 1) * 32],
                            XF32[:, t * 128:(t + 1) * 128],
                            RWT[:, dc * 32:(dc + 1) * 32],
                            start=first_mm, stop=(dc == 7 and t == NT - 1),
                            skip_group_check=True)
                    first_mm = False
                dmm = em.prod(im, 'pe')
                sl_logi.users.append(dmm) if dc < 7 else sl_logi.set(dmm)
                sl_xf.used(dmm)
                sl_rwt.used(dmm)
            esR.close()
            em.after('ac', [em.tail('pe')])
            sl_sc.wr(em, 'ac')
            i1 = AC.activation(SC[:], LOGI[:], AF.Sigmoid)
            d1 = em.prod(i1, 'ac')
            sl_logi.used(d1)
            sl_sc.set(d1)
            rowsl = Slot()
            rowsl.wr(em, 'sy')
            dr = dma(em, 'dsy', P['ROWS'][0:1, 0:32], IN['rbias'].ap())
            rowsl.set(dr)
            sl_trf.wr(em, 'pe')
            rowsl.rd(em, 'pe')
            im = MM(em, TRF[:, 0:32], P['onesf'][:], P['ROWS'][0:1, 0:32],
                    start=True, stop=True)
            dmm = em.prod(im, 'pe')
            sl_trf.set(dmm)
            rowsl.used(dmm)
            em.after('ve', [dmm, d1])
            sl_rbbc.wr(em, 've')
            em.prod(VE.tensor_copy(RBBC[:], TRF[:, 0:32]), 've')
            sl_trf.used(em.tail('ve'))
            sl_rbbc.set(em.tail('ve'))

            TW = TMPW.shape[1]
            T2W = TMP2.shape[1]

            sl_tmpw.wr(em, 've')
            em.prod(VE.tensor_tensor(
                bass.AP(TMPW, 0, [[TW, 128], [32, 16], [1, 32]]),
                bass.AP(SC, 0, [[SC.shape[1], 128], [32, 16], [1, 32]]),
                bass.AP(RBBC, 0, [[RBBC.shape[1], 128], [0, 16], [1, 32]]),
                ALU.add), 've')
            sl_sc.users.append(em.tail('ve'))
            sl_rbbc.used(em.tail('ve'))
            sl_tmpw.set(em.tail('ve'))
            em.prod(VE.tensor_reduce(TMP2[:, 0:64],
                                     bass.AP(TMPW, 0, [[TW, 128], [8, 64], [1, 8]]),
                                     AX.X, ALU.max), 've')
            em.vesync()
            sl_eqs.wr(em, 've')
            em.prod(VE.tensor_tensor(
                bass.AP(EQS, 0, [[EQS.shape[1], 128], [8, 64], [1, 8]]),
                bass.AP(TMPW, 0, [[TW, 128], [8, 64], [1, 8]]),
                bass.AP(TMP2, 0, [[T2W, 128], [1, 64], [0, 8]]),
                ALU.is_equal), 've')
            sl_eqs.set(em.tail('ve'))
            em.prod(VE.scalar_tensor_tensor(TMP3[:], EQS[:, 0:512], -1.0e9, TMPW[:],
                                            ALU.mult, ALU.add), 've')
            em.prod(VE.tensor_reduce(TMP2[:, 64:128],
                                     bass.AP(TMP3, 0, [[T2W, 128], [8, 64], [1, 8]]),
                                     AX.X, ALU.max), 've')
            em.vesync()
            em.prod(VE.tensor_tensor(TMP2[:, 0:64], TMP2[:, 0:64], TMP2[:, 64:128],
                                     ALU.add), 've')
            em.prod(VE.tensor_reduce(TMP2[:, 128:144],
                                     bass.AP(TMP2, 0, [[T2W, 128], [4, 16], [1, 4]]),
                                     AX.X, ALU.max), 've')
            em.vesync()
            em.prod(VE.tensor_tensor(
                bass.AP(TMP2, 192, [[T2W, 128], [4, 16], [1, 4]]),
                bass.AP(TMP2, 0, [[T2W, 128], [4, 16], [1, 4]]),
                bass.AP(TMP2, 128, [[T2W, 128], [1, 16], [0, 4]]),
                ALU.is_equal), 've')
            em.prod(VE.scalar_tensor_tensor(TMP2[:, 0:64], TMP2[:, 192:256], -1.0e9,
                                            TMP2[:, 0:64], ALU.mult, ALU.add), 've')
            em.prod(VE.tensor_reduce(TMP2[:, 144:160],
                                     bass.AP(TMP2, 0, [[T2W, 128], [4, 16], [1, 4]]),
                                     AX.X, ALU.max), 've')
            em.vesync()
            em.prod(VE.tensor_tensor(
                bass.AP(TMP2, 256, [[T2W, 128], [4, 16], [1, 4]]),
                bass.AP(TMP2, 0, [[T2W, 128], [4, 16], [1, 4]]),
                bass.AP(TMP2, 144, [[T2W, 128], [1, 16], [0, 4]]),
                ALU.is_equal), 've')
            em.prod(VE.tensor_tensor(TMP2[:, 192:256], TMP2[:, 192:256], TMP2[:, 256:320],
                                     ALU.add), 've')
            em.prod(VE.tensor_tensor(
                bass.AP(TMPW, 0, [[TW, 128], [8, 64], [1, 8]]),
                bass.AP(TMPW, 0, [[TW, 128], [8, 64], [1, 8]]),
                bass.AP(TMP2, 192, [[T2W, 128], [1, 64], [0, 8]]),
                ALU.mult), 've')
            for i in range(4):
                em.prod(VE.tensor_reduce(TMP2[:, 320:336],
                                         bass.AP(TMPW, 0, [[TW, 128], [32, 16], [1, 32]]),
                                         AX.X, ALU.max), 've')
                em.vesync()
                em.prod(VE.tensor_tensor(
                    bass.AP(EQS, i * 512, [[EQS.shape[1], 128], [32, 16], [1, 32]]),
                    bass.AP(TMPW, 0, [[TW, 128], [32, 16], [1, 32]]),
                    bass.AP(TMP2, 320, [[T2W, 128], [1, 16], [0, 32]]),
                    ALU.is_equal), 've')
                em.prod(VE.tensor_tensor(TMP2[:], EQS[:, i * 512:(i + 1) * 512], SC[:],
                                         ALU.mult), 've')
                em.prod(VE.tensor_reduce(W4[:, i * 16:(i + 1) * 16],
                                         bass.AP(TMP2, 0, [[T2W, 128], [32, 16], [1, 32]]),
                                         AX.X, ALU.add), 've')
                em.vesync()
                em.prod(VE.scalar_tensor_tensor(TMPW[:], EQS[:, i * 512:(i + 1) * 512],
                                                -1.0e9, TMPW[:], ALU.mult, ALU.add), 've')
            em.prod(VE.tensor_tensor(W4[:, 64:80], W4[:, 0:16], W4[:, 16:32], ALU.add), 've')
            em.prod(VE.tensor_tensor(W4[:, 64:80], W4[:, 64:80], W4[:, 32:48], ALU.add), 've')
            em.prod(VE.tensor_tensor(W4[:, 64:80], W4[:, 64:80], W4[:, 48:64], ALU.add), 've')
            em.prod(VE.tensor_scalar(W4[:, 64:80], W4[:, 64:80], 1.0e-20, None, ALU.add), 've')
            em.prod(VE.reciprocal(TMP2[:, 0:16], W4[:, 64:80]), 've')
            em.vesync()
            em.prod(VE.tensor_scalar(TMP2[:, 0:16], TMP2[:, 0:16], 2.5, None, ALU.mult), 've')
            for i in range(4):
                em.prod(VE.tensor_tensor(W4[:, i * 16:(i + 1) * 16], W4[:, i * 16:(i + 1) * 16],
                                         TMP2[:, 0:16], ALU.mult), 've')
                dst = WD if i == 0 else TMPW
                em.prod(VE.tensor_tensor(
                    bass.AP(dst, 0, [[dst.shape[1], 128], [32, 16], [1, 32]]),
                    bass.AP(EQS, i * 512, [[EQS.shape[1], 128], [32, 16], [1, 32]]),
                    bass.AP(W4, i * 16, [[W4.shape[1], 128], [1, 16], [0, 32]]),
                    ALU.mult), 've')
                if i > 0:
                    em.prod(VE.tensor_tensor(WD[:], WD[:], TMPW[:], ALU.add), 've')
            sl_wd.set(em.tail('ve'))
            if _DBG:
                dma(em, 'dgp', IN['dbg_sc'].ap(), SC[:], src_slots=(sl_sc,))
                dma(em, 'dgp', IN['dbg_wd'].ap(), WD[:], src_slots=(sl_wd,))

            def fetch_expert(e):
                i2_ = e % 2
                for s_, src_, shp in ((sl_gb[i2_], 'gateT', 0), (sl_ub[i2_], 'upT', 0)):
                    pass
                sl_gb[i2_].wr(em, 'gp')
                dma(em, 'dgp', GB[:, i2_ * 4096:(i2_ + 1) * 4096],
                    bass.AP(IN['gateT'], e * 1024 * 512, [[512, 128], [65536, 8], [1, 512]]))
                sl_gb[i2_].set(em.qtail('dgp'))
                sl_ub[i2_].wr(em, 'gp')
                dma(em, 'dgp', UB[:, i2_ * 4096:(i2_ + 1) * 4096],
                    bass.AP(IN['upT'], e * 1024 * 512, [[512, 128], [65536, 8], [1, 512]]))
                sl_ub[i2_].set(em.qtail('dgp'))

            fetch_expert(0)

            def expert_mats(e, gbuf, ubuf, dbuf, wcol, shared):
                for (buf, bsl, dst, dsl, act) in ((gbuf[0], gbuf[1], GSB, sl_gsb, True),
                                                  (ubuf[0], ubuf[1], USB, sl_usb, False)):
                    for ic in range(4):
                        for tc in range(4):
                            mp, msl = PG[tc % 2], sl_pg[tc % 2]
                            msl.wr(em, 'pe')
                            bsl.rd(em, 'pe')
                            for tt in range(4 * tc, 4 * tc + 4):
                                xt_slot[tt].rd(em, 'pe')
                            for dc in range(8):
                                im = MM(em, mp[:], buf[:, dc * 512 + ic * 128:dc * 512 + (ic + 1) * 128],
                                        XT[:, dc * T + tc * 512:dc * T + (tc + 1) * 512],
                                        start=(dc == 0), stop=(dc == 7))
                            dmm = em.prod(im, 'pe')
                            msl.set(dmm)
                            bsl.used(dmm)
                            for tt in range(4 * tc, 4 * tc + 4):
                                xt_slot[tt].used(dmm)
                            dsl.wr(em, 'ac')
                            msl.rd(em, 'ac')
                            ia = AC.activation(dst[:, ic * 2048 + tc * 512:ic * 2048 + (tc + 1) * 512],
                                               mp[:], AF.Silu if act else AF.Identity)
                            da = em.prod(ia, 'ac')
                            msl.used(da)
                            dsl.users.append(da)
                    dsl.set(em.tail('ac'))
                sl_gsb.wr(em, 've')
                sl_usb.rd(em, 've')
                for q in range(4):
                    iv = VE.tensor_tensor(GSB[:, q * 2048:(q + 1) * 2048],
                                          GSB[:, q * 2048:(q + 1) * 2048],
                                          USB[:, q * 2048:(q + 1) * 2048], ALU.mult)
                    em.prod(iv, 've')
                sl_usb.used(em.tail('ve'))
                sl_gsb.set(em.tail('ve'))
                for t in range(NT):
                    for half in range(2):
                        mp, msl = PD[half], sl_pd[half]
                        msl.wr(em, 'pe')
                        sl_gsb.rd(em, 'pe')
                        dbuf[1].rd(em, 'pe')
                        for ic in range(4):
                            im = MM(em, mp[:],
                                    GSB[:, ic * 2048 + t * 128:ic * 2048 + (t + 1) * 128],
                                    dbuf[0][:, ic * 1024 + half * 512:ic * 1024 + (half + 1) * 512],
                                    start=(ic == 0), stop=(ic == 3))
                        dmm = em.prod(im, 'pe')
                        msl.set(dmm)
                        sl_gsb.used(dmm)
                        dbuf[1].used(dmm)
                        em.after('ve', [dmm])
                        sl_acc[t].wr(em, 've')
                        accap = ACC[:, t * 1024 + half * 512:t * 1024 + (half + 1) * 512]
                        if shared:
                            iv = VE.tensor_tensor(accap, accap, mp[:], ALU.add)
                        else:
                            sl_wd.rd(em, 've')
                            iv = VE.scalar_tensor_tensor(accap, mp[:], WD[:, t * 32 + wcol:t * 32 + wcol + 1],
                                                         accap, ALU.mult, ALU.add)
                        dv = em.prod(iv, 've')
                        msl.used(dv)
                        if not shared:
                            sl_wd.used(dv)
                        sl_acc[t].set(dv)

            for e in range(32):
                if e < 31:
                    fetch_expert(e + 1)
                i2_ = e % 2
                sl_db.wr(em, 'gp')
                dma(em, 'dgp', DB[:],
                    bass.AP(IN['downT'], e * 512 * 1024, [[1024, 128], [131072, 4], [1, 1024]]))
                sl_db.set(em.qtail('dgp'))
                expert_mats(e, (GB[:, i2_ * 4096:(i2_ + 1) * 4096], sl_gb[i2_]),
                            (UB[:, i2_ * 4096:(i2_ + 1) * 4096], sl_ub[i2_]),
                            ((DB, sl_db))[0:2] if False else (DB, sl_db), e, False)

            for ih in range(2):
                i2_ = ih % 2
                sl_gb[i2_].wr(em, 'gp')
                dma(em, 'dgp', GB[:, i2_ * 4096:(i2_ + 1) * 4096],
                    bass.AP(IN['sgT'], ih * 512, [[1024, 128], [131072, 8], [1, 512]]))
                sl_gb[i2_].set(em.qtail('dgp'))
                sl_ub[i2_].wr(em, 'gp')
                dma(em, 'dgp', UB[:, i2_ * 4096:(i2_ + 1) * 4096],
                    bass.AP(IN['suT'], ih * 512, [[1024, 128], [131072, 8], [1, 512]]))
                sl_ub[i2_].set(em.qtail('dgp'))
                sl_db.wr(em, 'gp')
                dma(em, 'dgp', DB[:],
                    bass.AP(IN['sdT'], ih * 512 * 1024, [[1024, 128], [131072, 4], [1, 1024]]))
                sl_db.set(em.qtail('dgp'))
                expert_mats(32 + ih, (GB[:, i2_ * 4096:(i2_ + 1) * 4096], sl_gb[i2_]),
                            (UB[:, i2_ * 4096:(i2_ + 1) * 4096], sl_ub[i2_]),
                            (DB, sl_db), 0, True)

            sl_stat = Slot()
            esF = contextlib.ExitStack()
            FO = esF.enter_context(nc.sbuf_tensor('p4_fo', [128, 2 * 1024], F32))
            sl_fo = [Slot(), Slot()]
            for t in range(NT):
                tile = ACC[:, t * 1024:(t + 1) * 1024]
                sl_acc[t].rd(em, 've')
                sl_stat.wr(em, 've')
                em.prod(VE.bn_stats(P['STAT'][:, 0:6], tile[:, 0:512]), 've')
                em.vesync()
                em.prod(VE.bn_stats(P['STAT'][:, 6:12], tile[:, 512:1024]), 've')
                em.vesync()
                em.prod(VE.bn_aggr(P['STAT'][:, 12:14], P['STAT'][:, 0:12]), 've')
                em.vesync()
                _rstd(em, VE, AC, P['STAT'], 13, 14, (15, 16))
                em.prod(VE.tensor_tensor(P['STAT'][:, 15:16], P['STAT'][:, 12:13],
                                         P['STAT'][:, 14:15], ALU.mult), 've')
                em.prod(VE.tensor_scalar(P['STAT'][:, 15:16], P['STAT'][:, 15:16], -1.0,
                                         None, ALU.mult), 've')
                sl_stat.set(em.tail('ve'))
                em.after('ac', [em.tail('ve')])
                fo = FO[:, (t % 2) * 1024:(t % 2) * 1024 + 1024]
                sl_fo[t % 2].wr(em, 'ac')
                i1 = AC.activation(fo, tile, AF.Identity,
                                   bias=P['STAT'][:, 15:16], scale=P['STAT'][:, 14:15])
                d1 = em.prod(i1, 'ac')
                sl_stat.users.append(d1)
                sl_acc[t].used(d1)
                sl_fo[t % 2].set(d1)
                dma(em, 'dsy', OUT.ap()[t * 128:(t + 1) * 128, :], fo,
                    src_slots=(sl_fo[t % 2],))
            esF.close()
    es.close()



def kernel(**inputs):
    w = _prep_weights(inputs)
    hc = _host_constants()
    x = np.asarray(inputs['L0_attn_x'], dtype=np.float32)
    nc = build_nc()
    in_maps = []
    for c in range(8):
        m = {}
        m['x_tm'] = np.ascontiguousarray(x[c])
        m['x_dm'] = np.ascontiguousarray(x[c].T)
        for k, v in w.items():
            m[k] = np.ascontiguousarray(np.asarray(v, dtype=np.float32))
        for k, v in hc.items():
            m[k] = np.ascontiguousarray(np.asarray(v, dtype=np.float32))
        in_maps.append(m)
    res = run_bass_kernel_spmd(nc, in_maps, core_ids=list(range(8)))
    if _DBG:
        kernel._dbg = res.results
    out = np.stack([res.results[c]['out'] for c in range(8)], axis=0)
    return out.astype(np.float32)
kernel._dbg = None
